# version 20
# baseline (speedup 1.0000x reference)
import numpy as np
import ml_dtypes
from contextlib import ExitStack

import concourse.bass as bass
import concourse.mybir as mybir
from concourse.bass_utils import run_bass_kernel_spmd

F32 = mybir.dt.float32
BF16 = mybir.dt.bfloat16
AF = mybir.ActivationFunctionType
ALU = mybir.AluOpType
AX = mybir.AxisListType
bfdt = ml_dtypes.bfloat16

T = 2048
D = 1024
DI = 2048
NST = 16
DCONV = 4
DTR = 64
FF = 4096
NE = 8
NR = 16
NDC = D // 128
NIC = DI // 128
NTC = T // 128
NTG = 4
NFC = FF // 512
NQ = 4
QW = T // NQ
EPS_LN = 1e-5
EPS_RMS = 1e-6

MAX_PHASE = 7
DEBUG_TAPS = ()
SCAN_GP_N = {2, 5, 8, 11, 13, 15}


class Build:
    def __init__(self, debug_taps=()):
        self.taps = set(debug_taps)
        self.nc = bass.Bass("TRN2", target_bir_lowering=False, debug=False)
        self.ins = {}
        self.dbg = {}
        self.es = ExitStack()
        self.sems = {}

    def din(self, name, shape, dt):
        ap = self.nc.dram_tensor(name, list(shape), dt, kind="ExternalInput").ap()
        self.ins[name] = ap
        return ap

    def dout(self, name, shape, dt=F32):
        ap = self.nc.dram_tensor(name, list(shape), dt, kind="ExternalOutput").ap()
        self.dbg[name] = ap
        return ap

    def dram(self, name, shape, dt):
        return self.nc.dram_tensor(name, list(shape), dt, kind="Internal").ap()

    def sb(self, name, shape, dt, stack=None):
        return (stack or self.es).enter_context(
            self.nc.sbuf_tensor(name, shape, dt))

    def ps(self, name, shape, stack=None):
        return (stack or self.es).enter_context(
            self.nc.psum_tensor(name, shape, F32))

    def S(self, name):
        if name not in self.sems:
            self.sems[name] = [self.es.enter_context(self.nc.semaphore(name)), 0]
        return self.sems[name]

    def inc(self, name, inst, amt=1):
        s = self.S(name)
        s[1] += amt
        inst.then_inc(s[0], amt)
        return s[1]

    def V(self, name):
        return self.S(name)[1]

    def W(self, eng, name, v=None):
        s = self.S(name)
        eng.wait_ge(s[0], s[1] if v is None else v)

    def preload_set(self, sc, funcs):
        from concourse.hw_specs import get_activation_tables
        for idx, fs in enumerate(get_activation_tables(self.nc.m.arch).values()):
            if all(f in fs for f in funcs):
                if getattr(self, "_cur_set", None) == idx:
                    return None
                self._cur_set = idx
                return sc.add_instruction(mybir.InstLoadActFuncSet(
                    name=self.nc.get_next_instruction_name(),
                    act_func_set_id=idx, ins=[], outs=[]))
        raise ValueError(f"no table set contains {funcs}")

    def build(self):
        nc = self.nc
        with self.es:
            self.block = self.es.enter_context(nc.Block())
            self.declare_io()
            self.p0_consts()
            self.p1_ln_gate()
            if MAX_PHASE >= 2:
                self.p2_inproj_conv()
            if MAX_PHASE >= 7:
                self.g67 = ExitStack()
                self.p6_ffn_prep()
                self.p7_ffn_main()
                self.g67.close()
            elif MAX_PHASE >= 4:
                self.p4_init()
                for k in range(NQ * NIC):
                    self.p4_piece(k)
                self.p4_close()
            self.gE.close()
            self.gA.close()
            if MAX_PHASE >= 5:
                self.p5_outproj()
            if MAX_PHASE >= 7:
                self.p8_resid()
            self.finish()
            for st in [getattr(self, "g67", None), self.g4, self.gE,
                       self.gB, self.gA]:
                if st is not None:
                    try:
                        st.close()
                    except Exception:
                        pass
        return nc, self.ins, self.dbg

    def declare_io(self):
        d = self.din
        self.x_fm = d("x_fm", (D, T), F32)
        self.ipT = d("ipT", (D, 2 * DI), BF16)
        self.opT = d("opT", (DI, D), BF16)
        self.xpTp = d("xpTp", (DI, 112), BF16)
        self.dtTd = d("dtT", (DTR, DI), BF16)
        self.A_pack = d("A_pack", (128, NIC * NST), BF16)
        self.dcold = d("dcol", (128, NIC), F32)
        self.cbcold = d("cbcol", (128, NIC), F32)
        self.dtbcold = d("dtbcol", (128, NIC), F32)
        self.convwd = d("convw", (128, NIC * DCONV), F32)
        self.nwpd = d("nwp", (128, NDC), F32)
        self.gwTd = d("gwT", (D, NE), F32)
        self.w1T = d("w1T", (D, FF), F32)
        self.w3T = d("w3T", (D, FF), F32)
        self.w2T = d("w2T", (FF, D), F32)
        self.a1TA = d("a1TA", (D, 128), F32)
        self.a1TB = d("a1TB", (D, 128), F32)
        self.a3TA = d("a3TA", (D, 128), F32)
        self.a3TB = d("a3TB", (D, 128), F32)
        self.b1PA = d("b1PA", (128, FF), F32)
        self.b1PB = d("b1PB", (128, FF), F32)
        self.b3PA = d("b3PA", (128, FF), F32)
        self.b3PB = d("b3PB", (128, FF), F32)
        self.a2Td = d("a2T", (FF, 128), F32)
        self.b2PA = d("b2PA", (128, D), F32)
        self.b2PB = d("b2PB", (128, D), F32)
        self.selTd = d("selT", (NST, NST * 128), BF16)
        self.wbseld = d("wbsel", (NE, NE * 128), BF16)
        self.eye_bf = d("eye_bf", (128, 128), BF16)
        self.eye_f = d("eye_f", (128, 128), F32)
        self.ones_f = d("ones_f", (128, 1), F32)
        self.ones_b = d("ones_b", (128, 1), BF16)
        self.onesc_f = d("onesc_f", (1, 128), F32)
        self.onesc_b = d("onesc_b", (1, 128), BF16)
        self.epsd = d("epsv", (1, 4), F32)
        self.twosd = d("twos_f", (128, 1), F32)

        self.sp_xconv = self.dram("sp_xconv", [NIC, 128, T], BF16)
        self.sp_z = self.dram("sp_z", [NIC, 128, T], BF16)
        self.sp_y1 = self.dram("sp_y1", [NIC, 128, T], BF16)
        self.sp_tfm = self.dram("sp_tfm", [NDC, 128, T], BF16)
        self.sp_s = self.dram("sp_s", [NDC, 128, T], BF16)
        self.out_fm = self.nc.dram_tensor("out_fm", [D, T], F32,
                                          kind="ExternalOutput").ap()

        self.sp_bc = self.dram("sp_bc", [2, NST, T], BF16)
        self.sp_g = self.dram("sp_g", [NTG, NDC, 128, 512], BF16)
        self.sp_wfm = self.dram("sp_wfm", [NE, T], BF16)

        self.gA = ExitStack()
        self.gB = ExitStack()
        self.gE = ExitStack()
        self.g4 = ExitStack()

    def p0_consts(self):
        block, inc, V, W = self.block, self.inc, self.V, self.W
        sb = self.sb
        self.t_eyeb = sb("t_eyeb", [128, 128], BF16)
        self.t_eyef = sb("t_eyef", [128, 128], F32)
        self.t_onesf = sb("t_onesf", [128, 1], F32)
        self.t_onesb = sb("t_onesb", [128, 1], BF16)
        self.t_onescf = sb("t_onescf", [1, 128], F32)
        self.t_onescb = sb("t_onescb", [1, 128], BF16)
        self.t_nwp = sb("t_nwp", [128, NDC], F32)
        self.t_Af = sb("t_Af", [128, NIC * NST], F32)
        self.t_dcol = sb("t_dcol", [128, NIC], F32)
        self.t_cbcol = sb("t_cbcol", [128, NIC], F32)
        self.t_dtbcol = sb("t_dtbcol", [128, NIC], F32)
        self.t_convw = sb("t_convw", [128, NIC * DCONV], F32)
        self.t_dtT = sb("t_dtT", [DTR, DI], BF16)
        self.t_wtm = sb("t_wtm", [128, NTC * NE], F32)
        self.t_wbsel = sb("t_wbsel", [NE, NE * 128], BF16)
        self.t_gw = sb("t_gw", [128, NDC * NE], F32)
        self.t_eps = sb("t_eps", [1, 4], F32)
        self.t_twos = sb("t_twos", [128, 1], F32)
        self.t_dtraw = sb("t_dtraw", [DTR, T], BF16, self.gA)
        t_Apack = sb("t_Apack", [128, NIC * NST], BF16, self.gA)
        t_gwraw = sb("t_gwraw", [128, NDC * NE], F32, self.gB)
        self.t_xp = sb("t_xp", [128, NIC * 112], BF16, self.gB)
        self.t_sel = sb("t_sel", [NST, NST * 128], BF16, self.gB)

        @block.sync
        def _(sync):
            for src, dst in [(self.eye_bf, self.t_eyeb), (self.eye_f, self.t_eyef),
                             (self.ones_f, self.t_onesf), (self.ones_b, self.t_onesb),
                             (self.onesc_f, self.t_onescf),
                             (self.onesc_b, self.t_onescb),
                             (self.nwpd, self.t_nwp),
                             (self.A_pack, t_Apack), (self.dcold, self.t_dcol),
                             (self.cbcold, self.t_cbcol),
                             (self.dtbcold, self.t_dtbcol),
                             (self.convwd, self.t_convw), (self.dtTd, self.t_dtT),
                             (self.selTd, self.t_sel),
                             (self.wbseld, self.t_wbsel),
                             (self.epsd, self.t_eps), (self.twosd, self.t_twos)]:
                inc("ld", sync.dma_start(out=dst[:], in_=src[:]), 16)
            inc("ld", sync.dma_start(
                out=self.t_xp[:].rearrange("p (i c) -> p i c", i=NIC),
                in_=self.xpTp.rearrange("(i p) c -> p i c", i=NIC, p=128)), 16)
            inc("ld", sync.dma_start(
                out=t_gwraw[:].rearrange("p (i c) -> p i c", i=NDC),
                in_=self.gwTd.rearrange("(i p) c -> p i c", i=NDC, p=128)), 16)
        ld0 = V("ld")

        self.t_Apack = t_Apack
        self.ld0v = ld0

        @block.vector
        def _(ve):
            W(ve, "ld", ld0)
            for dc in range(NDC):
                inc("ve", ve.tensor_scalar(
                    self.t_gw[:, dc * NE:(dc + 1) * NE],
                    t_gwraw[:, dc * NE:(dc + 1) * NE],
                    self.t_nwp[:, dc:dc + 1], None, ALU.mult))
        self.ld0 = ld0

    def p1_ln_gate(self):
        block, inc, V, W = self.block, self.inc, self.V, self.W
        nc, sb = self.nc, self.sb
        g1 = ExitStack()
        self.t_xnb = [sb(f"t_xnb{i}", [128, T], BF16, self.gB) for i in range(NDC)]
        t_x = [sb(f"t_x{i}", [128, T], F32, g1) for i in range(NDC)]
        t_xn = t_x
        t_tf = [sb(f"t_tf{i}", [128, T], BF16, g1) for i in range(NDC)]
        t_x2 = [sb(f"t_x2{i}", [128, T], BF16, g1) for i in range(2)]
        _r = [sb(f"t_row{i}", [1, T], F32, g1) for i in range(4)]
        rows = {"m": _r[0], "m2": _r[1], "v": _r[1], "v2": _r[1],
                "sd": _r[2], "rs": _r[2], "sd2": _r[3], "rs2": _r[3]}
        t_logit = sb("t_logit", [NE, T], F32, g1)
        t_rs2tm = sb("t_rs2tm", [128, NTC], F32, g1)
        t_ltm = sb("t_ltm", [128, NTC * NE], F32, g1)
        t_l1pw = sb("t_l1pw", [128, NTC], F32, g1)
        t_l1n = sb("t_l1n", [128, NTC], F32, g1)
        t_l2cw = sb("t_l2cw", [128, NTC * NE], F32, g1)
        t_l2pw = sb("t_l2pw", [128, NTC], F32, g1)
        t_m1w = sb("t_m1w", [128, NTC * NE], F32, g1)
        t_m2w = sb("t_m2w", [128, NTC * NE], F32, g1)
        t_sel2w = sb("t_sel2w", [128, NTC * NE], F32, g1)
        t_qw = sb("t_qw", [128, NTC * NE], F32, g1)
        t_qsw = sb("t_qsw", [128, NTC * NE], F32, g1)
        t_ssumw = sb("t_ssumw", [128, NTC], F32, g1)
        t_rsumw = sb("t_rsumw", [128, NTC], F32, g1)

        @block.sync
        def _(sync):
            for dc in range(NDC):
                inc("lx", sync.dma_start(
                    out=t_x[dc][:], in_=self.x_fm[dc * 128:(dc + 1) * 128, :]), 16)
        ld_all = ("lx", V("lx"))

        with ExitStack() as pp:
            p_sum = self.ps("p_sum", [1, T], pp)
            p_sq = self.ps("p_sq", [1, T], pp)

            se_sq = []
            for dc in range(NDC):
                @block.scalar
                def _(sc, dc=dc):
                    if dc == 0:
                        self.preload_set(sc, [AF.Sqrt, AF.Square])
                    W(sc, *ld_all)
                    if dc >= 2:
                        W(sc, "te", te_sq[dc - 2])
                    inc("se", sc.activation(t_x2[dc % 2][:], t_x[dc][:], AF.Square))
                se_sq.append(V("se"))
                if dc == 0:
                    te_sq = []

                @block.tensor
                def _(te, dc=dc):
                    W(te, *ld_all)
                    if dc == 0:
                        W(te, "ld", self.ld0)
                    for k in range(4):
                        te.matmul(p_sum[:, k * 512:(k + 1) * 512], self.t_onesf[:],
                                  t_x[dc][:, k * 512:(k + 1) * 512],
                                  start=(dc == 0), stop=(dc == NDC - 1))
                    W(te, "se", se_sq[dc])
                    for k in range(4):
                        i_ = te.matmul(p_sq[:, k * 512:(k + 1) * 512], self.t_onesb[:],
                                       t_x2[dc % 2][:, k * 512:(k + 1) * 512],
                                       start=(dc == 0), stop=(dc == NDC - 1))
                    inc("te", i_)
                te_sq.append(V("te"))
            te_stats = V("te")

            @block.vector
            def _(ve):
                W(ve, "te", te_stats)
                inc("ve", ve.tensor_single_scalar(rows["m"][:], p_sum[:], 1.0 / D,
                                                  ALU.mult))
                W(ve, "ve")
                inc("ve", ve.tensor_mul(rows["m2"][:], rows["m"][:], rows["m"][:]))
                W(ve, "ve")
                inc("ve", ve.scalar_tensor_tensor(
                    rows["v"][:], p_sq[:], 1.0 / D, rows["m2"][:],
                    ALU.mult, ALU.subtract))
            ve_v = V("ve")

        @block.scalar
        def _(sc):
            W(sc, "ve", ve_v)
            inc("se", sc.activation(rows["sd"][:], rows["v"][:], AF.Sqrt,
                                    bias=self.t_eps[0:1, 0:1]))
        se_sd = V("se")

        @block.vector
        def _(ve):
            W(ve, "se", se_sd)
            inc("ve", ve.reciprocal(rows["rs"][:], rows["sd"][:]))
            W(ve, "ve")
            inc("ve", ve.tensor_mul(rows["v2"][:], rows["v"][:], rows["rs"][:]))
            W(ve, "ve")
            inc("ve", ve.tensor_mul(rows["v2"][:], rows["v2"][:], rows["rs"][:]))
        ve_v2 = V("ve")

        @block.scalar
        def _(sc):
            W(sc, "ve", ve_v2)
            inc("se", sc.activation(rows["sd2"][:], rows["v2"][:], AF.Sqrt,
                                    bias=self.t_eps[0:1, 1:2]))
        se_sd2 = V("se")

        @block.vector
        def _(ve):
            W(ve, "se", se_sd2)
            inc("ve", ve.reciprocal(rows["rs2"][:], rows["sd2"][:]))
        ve_rs2 = V("ve")

        with ExitStack() as pp:
            p_mbc = self.ps("p_mbc", [128, T], pp)
            p_rbc = self.ps("p_rbc", [128, T], pp)

            @block.tensor
            def _(te):
                W(te, "ve", ve_rs2)
                for k in range(4):
                    te.matmul(p_mbc[:, k * 512:(k + 1) * 512], self.t_onescf[:],
                              rows["m"][:, k * 512:(k + 1) * 512],
                              start=True, stop=True)
                for k in range(4):
                    i_ = te.matmul(p_rbc[:, k * 512:(k + 1) * 512], self.t_onescf[:],
                                   rows["rs"][:, k * 512:(k + 1) * 512],
                                   start=True, stop=True)
                inc("te", i_)
            te_bc = V("te")

            for dc in range(NDC):
                @block.vector
                def _(ve, dc=dc):
                    if dc == 0:
                        W(ve, "te", te_bc)
                    inc("ve", ve.tensor_sub(t_xn[dc][:], t_x[dc][:], p_mbc[:]))
                    W(ve, "ve")
                    inc("ve", ve.tensor_mul(t_xn[dc][:], t_xn[dc][:], p_rbc[:]))
                    W(ve, "ve")
                    inc("ve", ve.tensor_copy(self.t_xnb[dc][:], t_xn[dc][:]))
            ve_xn = V("ve")

        with ExitStack() as pp:
            p_log = self.ps("p_log", [NE, T], pp)

            @block.tensor
            def _(te):
                W(te, "ve", ve_xn)
                for dc in range(NDC):
                    for k in range(4):
                        i_ = te.matmul(p_log[:, k * 512:(k + 1) * 512],
                                       self.t_gw[:, dc * NE:(dc + 1) * NE],
                                       t_xn[dc][:, k * 512:(k + 1) * 512],
                                       start=(dc == 0), stop=(dc == NDC - 1))
                inc("te", i_)
            te_log = V("te")

            @block.scalar
            def _(sc):
                W(sc, "te", te_log)
                inc("se", sc.copy(t_logit[:], p_log[:]))
            se_log = V("se")

        with ExitStack() as pp:
            p_lt = self.ps("p_lt", [128, NE], pp)
            p_rt = self.ps("p_rt", [128, 1], pp)
            for tc in range(NTC):
                @block.tensor
                def _(te, tc=tc):
                    if tc == 0:
                        W(te, "se", se_log)
                    else:
                        W(te, "ve", V("ve"))
                    te.transpose(p_lt[:], t_logit[:, tc * 128:(tc + 1) * 128],
                                 self.t_eyef[0:NE, 0:NE])
                    i_ = te.transpose(p_rt[:],
                                      rows["rs2"][:, tc * 128:(tc + 1) * 128],
                                      self.t_eyef[0:1, 0:1])
                    inc("te", i_)
                te_t = V("te")

                @block.vector
                def _(ve, tc=tc, te_t=te_t):
                    W(ve, "te", te_t)
                    inc("ve", ve.tensor_copy(t_rs2tm[:, tc:tc + 1], p_rt[:]))
                    W(ve, "ve")
                    inc("ve", ve.tensor_scalar(
                        t_ltm[:, tc * NE:(tc + 1) * NE], p_lt[:],
                        t_rs2tm[:, tc:tc + 1], None, ALU.mult))

        for tc in range(NTC):
            s8 = slice(tc * NE, (tc + 1) * NE)
            c1 = slice(tc, tc + 1)

            @block.vector
            def _(ve, tc=tc, s8=s8, c1=c1):
                lt = t_ltm[:, s8]
                W(ve, "ve")
                inc("ve", ve.reduce_max(t_l1pw[:, c1], lt, axis=AX.X))
                inc("ve", ve.reduce_max(t_l1n[:, c1], lt, axis=AX.X,
                                        negate=True))
                W(ve, "ve")
                inc("ve", ve.tensor_single_scalar(t_m1w[:, s8], lt,
                                                  t_l1pw[:, c1], ALU.is_ge))
                W(ve, "ve")
                inc("ve", ve.scalar_tensor_tensor(t_l2cw[:, s8], t_m1w[:, s8],
                                                  -1e30, lt,
                                                  ALU.mult, ALU.add))
                W(ve, "ve")
                inc("ve", ve.reduce_max(t_l2pw[:, c1], t_l2cw[:, s8], axis=AX.X))
                W(ve, "ve")
                inc("ve", ve.tensor_single_scalar(t_m2w[:, s8], t_l2cw[:, s8],
                                                  t_l2pw[:, c1], ALU.is_ge))
                W(ve, "ve")
                inc("ve", ve.tensor_add(t_sel2w[:, s8], t_m1w[:, s8],
                                        t_m2w[:, s8]))
            ve_sel = V("ve")

            @block.scalar
            def _(sc, tc=tc, s8=s8, c1=c1, ve_sel=ve_sel):
                if tc == 0:
                    self.preload_set(sc, [AF.Exp])
                W(sc, "ve", ve_sel)
                inc("se", sc.activation(t_qw[:, s8], t_ltm[:, s8],
                                        AF.Exp, bias=t_l1n[:, c1]))
            se_q = V("se")

            @block.vector
            def _(ve, tc=tc, s8=s8, c1=c1, se_q=se_q):
                W(ve, "se", se_q)
                inc("ve", ve.tensor_mul(t_qsw[:, s8], t_qw[:, s8],
                                        t_sel2w[:, s8]))
                W(ve, "ve")
                inc("ve", ve.reduce_sum(t_ssumw[:, c1], t_qsw[:, s8], axis=AX.X))
                W(ve, "ve")
                inc("ve", ve.reciprocal(t_rsumw[:, c1], t_ssumw[:, c1]))
                W(ve, "ve")
                inc("ve", ve.tensor_scalar(
                    self.t_wtm[:, s8], t_qsw[:, s8],
                    t_rsumw[:, c1], None, ALU.mult))
        ve_w = V("ve")

        with ExitStack() as pp:
            p_r2 = self.ps("p_r2", [128, T], pp)

            @block.tensor
            def _(te):
                W(te, "ve", ve_w)
                for k in range(4):
                    i_ = te.matmul(p_r2[:, k * 512:(k + 1) * 512], self.t_onescf[:],
                                   rows["rs2"][:, k * 512:(k + 1) * 512],
                                   start=True, stop=True)
                inc("te", i_)
            te_r2 = V("te")

            for dc in range(NDC):
                @block.vector
                def _(ve, dc=dc):
                    if dc == 0:
                        W(ve, "te", te_r2)
                    inc("ve", ve.scalar_tensor_tensor(
                        t_tf[dc][:], t_xn[dc][:], self.t_nwp[:, dc:dc + 1],
                        p_r2[:], ALU.mult, ALU.mult))
            ve_tf = V("ve")

            @block.sync
            def _(sync):
                W(sync, "ve", ve_tf)
                for dc in range(NDC):
                    inc("st", sync.dma_start(out=self.sp_tfm[dc], in_=t_tf[dc][:]),
                        16)
            st_tf = V("st")

        if "xn" in self.taps:
            o = self.dout("dbg_xn", (NDC, 128, T))

            @block.sync
            def _(sync):
                W(sync, "ve", ve_tf)
                for dc in range(NDC):
                    inc("sdbg", sync.dma_start(out=o[dc], in_=t_xn[dc][:]), 16)
        if "wtm" in self.taps:
            o = self.dout("dbg_wtm", (128, NTC * NE))
            o2 = self.dout("dbg_ltm", (128, NTC * NE))
            o3 = self.dout("dbg_logit", (NE, T))
            o4 = self.dout("dbg_rs2tm", (128, NTC))
            extra = {}
            for nm, tt in [("l1p", t_l1pw), ("l1n", t_l1n), ("m1", t_m1w),
                           ("l2c", t_l2cw), ("l2p", t_l2pw), ("m2", t_m2w),
                           ("sel2", t_sel2w), ("q", t_qw), ("qs", t_qsw),
                           ("ssum", t_ssumw), ("rsum", t_rsumw)]:
                extra[nm] = (self.dout("dbg_" + nm, tuple(tt.shape)), tt)

            @block.sync
            def _(sync):
                W(sync, "ve", ve_w)
                inc("sdbg", sync.dma_start(out=o[:], in_=self.t_wtm[:]), 16)
                inc("sdbg", sync.dma_start(out=o2[:], in_=t_ltm[:]), 16)
                inc("sdbg", sync.dma_start(out=o3[:], in_=t_logit[:]), 16)
                inc("sdbg", sync.dma_start(out=o4[:], in_=t_rs2tm[:]), 16)
                for nm, (oo, tt) in extra.items():
                    inc("sdbg", sync.dma_start(out=oo[:], in_=tt[:]), 16)
        if "tfm" in self.taps:
            o = self.dout("dbg_tfm", (NDC, 128, T), BF16)

            @block.sync
            def _(sync):
                W(sync, "st", st_tf)
                for dc in range(NDC):
                    inc("sdbg", sync.dma_start(out=o[dc], in_=self.sp_tfm[dc]), 16)

        @block.vector
        def _(ve):
            W(ve, "st", st_tf)
            if "sdbg" in self.sems:
                W(ve, "sdbg", V("sdbg"))
        g1.close()

    def p2_inproj_conv(self):
        block, inc, V, W = self.block, self.inc, self.V, self.W
        sb = self.sb
        t_Bfm = sb("t_Bfm", [NST, T], BF16, self.gB)
        t_Cfm = sb("t_Cfm", [NST, T], BF16, self.gB)
        g2 = ExitStack()
        t_slab = [sb(f"t_slab{i}", [128, 1024], BF16, g2) for i in range(2)]
        t_xin = [sb(f"t_xin{i}", [128, T + 3], BF16, g2) for i in range(2)]
        t_acc = [sb(f"t_acc{i}", [128, T], F32, g2) for i in range(2)]
        t_xcv = [sb(f"t_xcv{i}", [128, T], BF16, g2) for i in range(2)]
        t_zz = [sb(f"t_zz{i}", [128, T], BF16, g2) for i in range(2)]

        gproj = ExitStack()
        p_proj = self.ps("p_proj", [112, T], gproj)
        p_xz = [self.ps(f"p_xz{i}", [128, 1024], g2) for i in range(2)]

        @block.vector
        def _(ve):
            inc("ve", ve.memset(t_xin[0][:, 0:3], 0.0))
            inc("ve", ve.memset(t_xin[1][:, 0:3], 0.0))
        ve_pad = V("ve")

        te_consumed_slab = {}
        se_half = {}
        cons_done = {}
        for ic in range(2 * NIC):
            bb = ic % 2

            @block.sync
            def _(sync, ic=ic, bb=bb):
                if ic < 2 and "sdbg" in self.sems:
                    W(sync, "sdbg", V("sdbg"))
                if ic >= 2:
                    W(sync, "te", te_consumed_slab[ic - 2])
                inc(f"lw{bb}", sync.dma_start(
                    out=t_slab[bb][:].rearrange("p (i c) -> p i c", i=NDC),
                    in_=self.ipT.rearrange("(i p) c -> p i c", i=NDC, p=128)[
                        :, :, ic * 128:(ic + 1) * 128]), 16)
            ld_slab = V(f"lw{bb}")

            for half in range(2):
                @block.tensor
                def _(te, ic=ic, bb=bb, half=half, ld_slab=ld_slab):
                    W(te, f"lw{bb}", ld_slab)
                    if ic >= 1:
                        W(te, "se", se_half[(ic - 1, half)])
                    if ic < NIC and half == 0:
                        if ic >= 2:
                            W(te, "ve", cons_done[ic - 2])
                    if ic >= NIC and half == 0 and ic - 2 >= NIC:
                        W(te, *cons_done[ic - 2])
                    for dc in range(NDC):
                        for k in range(2):
                            i_ = te.matmul(
                                p_xz[half][:, k * 512:(k + 1) * 512],
                                t_slab[bb][:, dc * 128:(dc + 1) * 128],
                                self.t_xnb[dc][:, half * 1024 + k * 512:
                                               half * 1024 + (k + 1) * 512],
                                start=(dc == 0), stop=(dc == NDC - 1))
                    inc("te", i_)
                te_mm = V("te")

                @block.scalar
                def _(sc, ic=ic, bb=bb, half=half, te_mm=te_mm):
                    if ic == 0 and half == 0:
                        W(sc, "ve", ve_pad)
                    W(sc, "te", te_mm)
                    if ic < NIC:
                        dst = t_xin[bb][:, 3 + half * 1024:3 + (half + 1) * 1024]
                    else:
                        dst = t_zz[bb][:, half * 1024:(half + 1) * 1024]
                    inc("se", sc.copy(dst, p_xz[half][:]))
                se_half[(ic, half)] = V("se")
            te_consumed_slab[ic] = V("te")

            if ic < NIC:
                @block.vector
                def _(ve, ic=ic, bb=bb):
                    W(ve, "se", se_half[(ic, 1)])
                    if ic >= 2:
                        W(ve, "se", se_silu[ic - 2])
                    xin, acc = t_xin[bb], t_acc[bb]
                    inc("ve", ve.tensor_scalar(
                        acc[:], xin[:, 3:3 + T],
                        self.t_convw[:, ic * DCONV + 3:ic * DCONV + 4], None,
                        ALU.mult))
                    for k in (2, 1, 0):
                        W(ve, "ve")
                        inc("ve", ve.scalar_tensor_tensor(
                            acc[:], xin[:, k:k + T],
                            self.t_convw[:, ic * DCONV + k:ic * DCONV + k + 1],
                            acc[:], ALU.mult, ALU.add))
                ve_conv = V("ve")
                cons_done[ic] = ve_conv
                if ic == 0:
                    se_silu = {}

                @block.scalar
                def _(sc, ic=ic, bb=bb, ve_conv=ve_conv):
                    if ic == 0:
                        self.preload_set(sc, [AF.Silu])
                    W(sc, "ve", ve_conv)
                    if ic >= 2:
                        W(sc, *gd_xcv[ic - 2])
                        W(sc, "te", te_xp[ic - 2])
                    inc("se", sc.activation(t_xcv[bb][:], t_acc[bb][:], AF.Silu,
                                            bias=self.t_cbcol[:, ic:ic + 1]))
                se_silu[ic] = V("se")
                if ic == 0:
                    gd_xcv, te_xp = {}, {}

                @block.gpsimd
                def _(gp, ic=ic, bb=bb):
                    W(gp, "se", se_silu[ic])
                    inc(f"sxc{bb}", gp.dma_start(out=self.sp_xconv[ic],
                                                 in_=t_xcv[bb][:]), 16)
                gd_xcv[ic] = (f"sxc{bb}", V(f"sxc{bb}"))

                @block.tensor
                def _(te, ic=ic, bb=bb):
                    W(te, "se", se_silu[ic])
                    for k in range(4):
                        i_ = te.matmul(
                            p_proj[:, k * 512:(k + 1) * 512],
                            self.t_xp[:, ic * 112:(ic + 1) * 112],
                            t_xcv[bb][:, k * 512:(k + 1) * 512],
                            start=(ic == 0), stop=(ic == NIC - 1))
                    inc("te", i_)
                te_xp[ic] = V("te")
            else:
                @block.gpsimd
                def _(gp, ic=ic, bb=bb):
                    W(gp, "se", se_half[(ic, 1)])
                    inc(f"sz{bb}", gp.dma_start(out=self.sp_z[ic - NIC],
                                                in_=t_zz[bb][:]), 16)
                cons_done[ic] = (f"sz{bb}", V(f"sz{bb}"))
        te_proj = te_xp[NIC - 1]

        @block.scalar
        def _(sc):
            W(sc, "te", te_proj)
            inc("se", sc.copy(self.t_dtraw[:], p_proj[0:64, :]))
            inc("se", sc.copy(t_Bfm[:], p_proj[64:80, :]))
            inc("se", sc.copy(t_Cfm[:], p_proj[96:112, :]))
        se_proj = V("se")
        self.se_proj = se_proj

        @block.sync
        def _(sync):
            W(sync, "se", se_proj)
            inc("st", sync.dma_start(out=self.sp_bc[0], in_=t_Bfm[:]), 16)
            inc("st", sync.dma_start(out=self.sp_bc[1], in_=t_Cfm[:]), 16)
        self.st_bc = V("st")
        self.spill_p2 = [(nm, V(nm)) for nm in
                         ["sxc0", "sxc1", "sz0", "sz1"]]
        g2.close()
        gproj.close()
        self.gB.close()

        if "xconv" in self.taps:
            o = self.dout("dbg_xconv", (NIC, 128, T), BF16)

            @block.sync
            def _(sync):
                for nm, v in self.spill_p2:
                    W(sync, nm, v)
                for ic in range(NIC):
                    inc("sdbg", sync.dma_start(out=o[ic], in_=self.sp_xconv[ic]), 16)
        if "z" in self.taps:
            o = self.dout("dbg_z", (NIC, 128, T), BF16)

            @block.sync
            def _(sync):
                for nm, v in self.spill_p2:
                    W(sync, nm, v)
                for ic in range(NIC):
                    inc("sdbg", sync.dma_start(out=o[ic], in_=self.sp_z[ic]), 16)
        if "proj" in self.taps:
            o1 = self.dout("dbg_dtraw", (DTR, T), BF16)

            @block.sync
            def _(sync):
                W(sync, "se", se_proj)
                inc("sdbg", sync.dma_start(out=o1[:], in_=self.t_dtraw[:]), 16)

    def p4_init(self):
        from types import SimpleNamespace
        block, inc, V, W = self.block, self.inc, self.V, self.W
        sb = self.sb
        g4 = self.g4
        P = SimpleNamespace()
        self.P4 = P
        P.t_et = sb("t_et", [128, QW], BF16, g4)
        P.t_delta = [sb(f"t_delta{i}", [128, QW], BF16, g4) for i in range(2)]
        P.t_du = [sb(f"t_du{i}", [128, QW], BF16, g4) for i in range(2)]
        P.t_xcr = [sb(f"t_xcr{i}", [128, QW], BF16, g4) for i in range(2)]
        P.t_a = [sb(f"t_a{i}", [128, QW + 2], BF16, g4)
                 for i in range(NST)]
        P.t_b = [sb("t_b0", [128, QW + 2], BF16, g4)] * 2
        P.t_h = [sb(f"t_h{i}", [128, QW + 2], BF16, g4) for i in range(2)]
        P.t_hC = [sb(f"t_hC{i}", [128, QW], BF16, g4) for i in range(2)]
        P.t_y1 = [sb(f"t_y1{i}", [128, QW], BF16, g4) for i in range(2)]
        P.t_carry = sb("t_carry", [128, NIC * NST], BF16, g4)
        P.t_Bq = [sb(f"t_Bq{n}", [128, QW], BF16, g4) for n in range(NST)]
        P.t_Cq = [sb(f"t_Cq{n}", [128, QW], BF16, g4) for n in range(NST)]

        P.dbg_delta = None
        if "delta" in self.taps:
            P.dbg_delta = self.dout("dbg_delta", (NIC, 128, T), BF16)

        def bc_pair(sync, q, n):
            inc("bcq", sync.dma_start(
                out=P.t_Bq[n][:],
                in_=self.sp_bc[0][n:n + 1, q * QW:(q + 1) * QW]
                .broadcast_to((128, QW))), 16)
            inc("bcq", sync.dma_start(
                out=P.t_Cq[n][:],
                in_=self.sp_bc[1][n:n + 1, q * QW:(q + 1) * QW]
                .broadcast_to((128, QW))), 16)
        P.bc_pair = bc_pair

        @block.sync
        def _(sync):
            W(sync, "st", self.st_bc)
            W(sync, "se", self.se_proj)
            for nm, v in self.spill_p2:
                W(sync, nm, v)
            for n in range(NST):
                bc_pair(sync, 0, n)
        P.bcq_full = {0: V("bcq")}

        @block.vector
        def _(ve):
            W(ve, "st", self.st_bc)
            for nm, v in self.spill_p2:
                W(ve, nm, v)
            for i in range(NST):
                inc("ve", ve.memset(P.t_a[i][:, 1:2], 1.0))
        P.ve_ainit = V("ve")

        P.pp = ExitStack()
        P.p_dl = self.ps("p_dl", [128, QW], P.pp)
        P.p_y = self.ps("p_y", [128, QW], P.pp)
        P.se_et, P.se_dl, P.ve_du, P.ve_y1, P.st_y1 = {}, {}, {}, {}, {}
        P.se_a, P.ve_b, P.ve_scan, P.gp_hC, P.te_y = {}, {}, {}, {}, {}
        P.ldx, P.te_dl = {}, {}
        P.kv, P.nv, P.pend, P.credits = 0, 0, None, 0.0

    def p4_piece(self, k):
        self.p4_xcr(k)
        self.p4_dt(k)
        self.p4_exps(k)
        for n in range(NST):
            self.p4_ngroup(k, n)
            if n >= 1:
                self.p4_ngroup_te(k, n - 1)
        self.p4_ngroup_te(k, NST - 1)
        self.p4_y1(k)

    def p4_xcr(self, k):
        block, inc, V, W = self.block, self.inc, self.V, self.W
        P = self.P4
        q, ic = divmod(k, NIC)
        icb = ic % 2

        @block.sync
        def _(sync, q=q, ic=ic, icb=icb):
            pv = ((q, ic - 2) if ic >= 2 else
                  (q - 1, NIC - 2 + ic) if q >= 1 else None)
            if pv is not None:
                W(sync, "ve", P.ve_y1[pv])
            else:
                for nm, v in self.spill_p2:
                    W(sync, nm, v)
            inc(f"lxc{icb}", sync.dma_start(
                out=P.t_xcr[icb][:],
                in_=self.sp_xconv[ic][:, q * QW:(q + 1) * QW]), 16)
        P.ldx[(q, ic)] = (f"lxc{icb}", V(f"lxc{icb}"))

    def p4_dt(self, k):
        block, inc, V, W = self.block, self.inc, self.V, self.W
        P = self.P4
        q, ic = divmod(k, NIC)

        @block.tensor
        def _(te, q=q, ic=ic):
            pv = ((q, ic - 1) if ic >= 1 else
                  (q - 1, NIC - 1) if q >= 1 else None)
            if pv is None:
                W(te, "se", self.se_proj)
            else:
                W(te, "se", P.se_et[pv])
            i_ = te.matmul(P.p_dl[:],
                           self.t_dtT[:, ic * 128:(ic + 1) * 128],
                           self.t_dtraw[:, q * QW:(q + 1) * QW],
                           start=True, stop=True)
            inc("te", i_)
        P.te_dl[(q, ic)] = V("te")

    def p4_exps(self, k):
        self.p4_exps_a(k)
        self.p4_exps_b(k)

    def p4_exps_a(self, k):
        block, inc, V, W = self.block, self.inc, self.V, self.W
        P = self.P4
        q, ic = divmod(k, NIC)
        icb = ic % 2
        se_dl, ve_du, ve_scan = P.se_dl, P.ve_du, P.ve_scan

        @block.scalar
        def _(sc, q=q, ic=ic, icb=icb):
            self.preload_set(sc, [AF.Exp, AF.Ln])
            if q == 0 and ic == 0:
                inc("se", sc.activation(self.t_Af[:],
                                        self.t_Apack[:], AF.Exp))
            W(sc, "te", P.te_dl[(q, ic)])
            inc("se", sc.activation(
                P.t_et[:], P.p_dl[:], AF.Exp,
                bias=self.t_dtbcol[:, ic:ic + 1]))
        P.se_et[(q, ic)] = V("se")

        if q == 0 and ic == 0:
            @block.vector
            def _(ve):
                W(ve, "se", P.se_et[(0, 0)])
                W(ve, "ve", P.ve_ainit)
                inc("ve", ve.tensor_single_scalar(
                    self.t_Af[:], self.t_Af[:], -1.0, ALU.mult))
            P.ve_neg = V("ve")

        @block.scalar
        def _(sc, q=q, ic=ic, icb=icb):
            pv = ((q, ic - 2) if ic >= 2 else
                  (q - 1, NIC - 2 + ic) if q >= 1 else None)
            if pv is not None:
                W(sc, "ve", ve_du[pv])
            W(sc, "se")
            inc("se", sc.activation(P.t_delta[icb][:], P.t_et[:],
                                    AF.Ln, bias=1.0))
        se_dl[(q, ic)] = V("se")

        if P.dbg_delta is not None:
            @block.sync
            def _(sync, q=q, ic=ic, icb=icb):
                W(sync, "se", se_dl[(q, ic)])
                inc("sdbg", sync.dma_start(
                    out=P.dbg_delta[ic][:, q * QW:(q + 1) * QW],
                    in_=P.t_delta[icb][:]), 16)
            sdbg_now = V("sdbg")

            @block.scalar
            def _(sc, sdbg_now=sdbg_now):
                W(sc, "sdbg", sdbg_now)

        self._p4_a_exps(k, 0, 7)

    def p4_exps_b(self, k):
        self._p4_a_exps(k, 7, NST)

    def _p4_a_exps(self, k, n0, n1):
        block, inc, V, W = self.block, self.inc, self.V, self.W
        P = self.P4
        q, ic = divmod(k, NIC)
        icb = ic % 2
        se_dl, ve_scan = P.se_dl, P.ve_scan
        for n in range(n0, n1):
            pvp = ((q, ic - 1, n) if ic >= 1 else
                   (q - 1, NIC - 1, n) if q >= 1 else None)

            @block.scalar
            def _(sc, q=q, ic=ic, n=n, icb=icb, pvp=pvp):
                self.preload_set(sc, [AF.Exp, AF.Ln])
                if q == 0 and ic == 0 and n == 0:
                    W(sc, "ve", P.ve_neg)
                if pvp is not None:
                    W(sc, "ve", ve_scan[pvp])
                W(sc, "se", se_dl[(q, ic)])
                inc("se", sc.activation(
                    P.t_a[n][:, 2:QW + 2], P.t_delta[icb][:], AF.Exp,
                    scale=self.t_Af[:,
                                    ic * NST + n:ic * NST + n + 1]))
            P.se_a[(q, ic, n)] = V("se")

    def p4_du(self, k):
        block, inc, V, W = self.block, self.inc, self.V, self.W
        P = self.P4
        q, ic = divmod(k, NIC)
        icb = ic % 2

        @block.vector
        def _(ve, q=q, ic=ic, icb=icb):
            W(ve, "se", P.se_dl[(q, ic)])
            W(ve, *P.ldx[(q, ic)])
            inc("ve", ve.tensor_mul(P.t_du[icb][:], P.t_delta[icb][:],
                                    P.t_xcr[icb][:]))
        P.ve_du[(q, ic)] = V("ve")

    def p4_ngroup(self, k, n):
        block, inc, V, W = self.block, self.inc, self.V, self.W
        P = self.P4
        q, ic = divmod(k, NIC)
        icb = ic % 2
        if n == 0:
            self.p4_du(k)
        nb2 = n % 2
        t_a, t_b, t_h, t_hC = P.t_a, P.t_b, P.t_h, P.t_hC
        t_carry, t_Bq, t_Cq = P.t_carry, P.t_Bq, P.t_Cq
        ve_scan, gp_hC, te_y = P.ve_scan, P.gp_hC, P.te_y
        pv2 = ((q, ic, n - 2) if n >= 2 else
               (q, ic - 1, NST - 2 + n) if ic >= 1 else
               (q - 1, NIC - 1, NST - 2 + n) if q >= 1
               else None)

        @block.vector
        def _(ve, q=q, ic=ic, n=n, icb=icb, nb2=nb2):
            if ic == 0 and n == 0:
                W(ve, "bcq", P.bcq_full[q])
            if n == 0:
                W(ve, "ve")
            inc("ve", ve.tensor_mul(t_b[nb2][:, 2:QW + 2],
                                    P.t_du[icb][:], t_Bq[n][:]))
            if q == 0:
                inc("ve", ve.memset(t_b[nb2][:, 1:2], 0.0))
            else:
                inc("ve", ve.tensor_copy(
                    t_b[nb2][:, 1:2],
                    t_carry[:, ic * NST + n:ic * NST + n + 1]))
        P.ve_b[(q, ic, n)] = V("ve")

        @block.vector
        def _(ve, q=q, ic=ic, n=n, nb2=nb2):
            W(ve, "se", P.se_a[(q, ic, n)])
            W(ve, "ve")
            inc("ve", ve.tensor_tensor_scan(
                t_h[nb2][:, 1:QW + 2], t_a[n][:, 1:QW + 2],
                t_b[nb2][:, 1:QW + 2], 0.0,
                ALU.mult, ALU.add))
            if q < NQ - 1:
                W(ve, "ve")
                inc("ve", ve.tensor_copy(
                    t_carry[:, ic * NST + n:ic * NST + n + 1],
                    t_h[nb2][:, QW + 1:QW + 2]))
        ve_scan[(q, ic, n)] = V("ve")

        @block.vector
        def _(ve, q=q, ic=ic, n=n, nb2=nb2, pv2=pv2):
            if pv2 is not None:
                W(ve, "te", te_y[pv2])
            if q == NQ - 1:
                W(ve, "ve")
            inc("ve", ve.tensor_mul(t_hC[nb2][:],
                                    t_h[nb2][:, 2:QW + 2],
                                    t_Cq[n][:]))
        gp_hC[(q, ic, n)] = V("ve")

        if ic == NIC - 1 and q < NQ - 1:
            @block.sync
            def _(sync, q=q, n=n):
                W(sync, "ve", gp_hC[(q, NIC - 1, n)])
                P.bc_pair(sync, q + 1, n)
            if n == NST - 1:
                P.bcq_full[q + 1] = V("bcq")

    def p4_ngroup_te(self, k, n):
        block, inc, V, W = self.block, self.inc, self.V, self.W
        P = self.P4
        q, ic = divmod(k, NIC)
        nb2 = n % 2
        te_y = P.te_y

        @block.tensor
        def _(te, q=q, ic=ic, n=n, nb2=nb2):
            W(te, "ve", P.gp_hC[(q, ic, n)])
            if n == 0:
                pv = ((q, ic - 1) if ic >= 1 else
                      (q - 1, NIC - 1) if q >= 1 else None)
                if pv is not None:
                    W(te, "ve", P.ve_y1[pv])
            i_ = te.matmul(P.p_y[:], self.t_eyeb[:],
                           P.t_hC[nb2][:],
                           start=(n == 0), stop=(n == NST - 1))
            inc("te", i_)
        te_y[(q, ic, n)] = V("te")

    def p4_y1(self, k):
        block, inc, V, W = self.block, self.inc, self.V, self.W
        P = self.P4
        q, ic = divmod(k, NIC)
        icb = ic % 2

        @block.vector
        def _(ve, q=q, ic=ic, icb=icb):
            yb = P.t_y1[icb]
            pv = ((q, ic - 2) if ic >= 2 else
                  (q - 1, NIC - 2 + ic) if q >= 1 else None)
            if pv is not None:
                W(ve, *P.st_y1[pv])
            inc("ve", ve.tensor_scalar(
                yb[:], P.t_xcr[icb][:], self.t_dcol[:, ic:ic + 1],
                None, ALU.mult))
            W(ve, "te", P.te_y[(q, ic, NST - 1)])
            W(ve, "ve")
            inc("ve", ve.scalar_tensor_tensor(
                yb[:], P.p_y[:], 1.0, yb[:], ALU.mult, ALU.add))
        P.ve_y1[(q, ic)] = V("ve")

        @block.gpsimd
        def _(gp, q=q, ic=ic, icb=icb):
            W(gp, "ve", P.ve_y1[(q, ic)])
            inc(f"sy{icb}", gp.dma_start(
                out=self.sp_y1[ic][:, q * QW:(q + 1) * QW],
                in_=P.t_y1[icb][:]), 16)
        P.st_y1[(q, ic)] = (f"sy{icb}", V(f"sy{icb}"))

    def _p4_step(self):
        P = self.P4
        if P.pend is not None:
            pk, pn = P.pend
            self.p4_ngroup_te(pk, pn)
            if pn == NST - 1:
                self.p4_y1(pk)
                if pk + 1 < NQ * NIC:
                    self.p4_exps_b(pk + 1)
            P.pend = None
        if P.kv >= NQ * NIC:
            return False
        self.p4_ngroup(P.kv, P.nv)
        P.pend = (P.kv, P.nv)
        if P.nv == 7 and P.kv + 1 < NQ * NIC:
            self.p4_xcr(P.kv + 1)
            self.p4_dt(P.kv + 1)
            self.p4_exps_a(P.kv + 1)
        P.nv += 1
        if P.nv == NST:
            P.nv = 0
            P.kv += 1
        return True

    def p4_advance(self, credit):
        P = self.P4
        P.credits += credit
        while P.credits >= 1.0:
            P.credits -= 1.0
            if not self._p4_step():
                break

    def p4_close(self):
        block, inc, V, W = self.block, self.inc, self.V, self.W
        while self._p4_step():
            pass
        self.P4.pp.close()
        self.g4.close()
        if "y1" in self.taps:
            o = self.dout("dbg_y1", (NIC, 128, T), BF16)

            @block.sync
            def _(sync):
                W(sync, "sy0", V("sy0"))
                W(sync, "sy1", V("sy1"))
                for ic in range(NIC):
                    inc("sdbg", sync.dma_start(out=o[ic], in_=self.sp_y1[ic]), 16)

    def p5_outproj(self):
        block, inc, V, W = self.block, self.inc, self.V, self.W
        sb = self.sb
        g5 = ExitStack()
        t_oslab = [sb(f"t_oslab{i}", [128, 1024], BF16, g5) for i in range(2)]
        t_y1r = [sb(f"t_y1r{i}", [128, 512], BF16, g5) for i in range(2)]
        t_zr = [sb(f"t_zr{i}", [128, 512], BF16, g5) for i in range(2)]
        t_sz = [sb(f"t_sz{i}", [128, 512], BF16, g5) for i in range(2)]
        t_y2 = [sb(f"t_y2{i}", [128, 512], BF16, g5) for i in range(2)]
        t_sev = [sb(f"t_sev{i}", [128, 512], BF16, g5) for i in range(NDC)]
        p_s = [self.ps(f"p_s{i}", [128, 512], g5) for i in range(NDC)]

        sy_all = [("sy0", V("sy0")), ("sy1", V("sy1")),
                  ("sz0", V("sz0")), ("sz1", V("sz1"))]
        se_sev = 0
        ve_y2t, te_op = {}, {}
        for tb in range(4):
            for ic in range(NIC):
                j = ic % 2
                step = tb * NIC + ic

                @block.sync
                def _(sync, tb=tb, ic=ic, j=j, step=step):
                    if step == 0:
                        for nm, v in sy_all:
                            W(sync, nm, v)
                    if step >= 2:
                        W(sync, "ve", ve_y2t[step - 2])
                    inc(f"lyz{j}", sync.dma_start(
                        out=t_y1r[j][:],
                        in_=self.sp_y1[ic][:, tb * 512:(tb + 1) * 512]), 16)
                    inc(f"lyz{j}", sync.dma_start(
                        out=t_zr[j][:],
                        in_=self.sp_z[ic][:, tb * 512:(tb + 1) * 512]), 16)
                ldx_now = (f"lyz{j}", V(f"lyz{j}"))

                @block.sync
                def _(sync, ic=ic, j=j, step=step):
                    if step < 2:
                        for nm, v in sy_all:
                            W(sync, nm, v)
                    else:
                        W(sync, "te", te_op[step - 2])
                    inc(f"lo{j}", sync.dma_start(
                        out=t_oslab[j][:],
                        in_=self.opT[ic * 128:(ic + 1) * 128, :]), 16)
                ldw_now = (f"lo{j}", V(f"lo{j}"))

                @block.scalar
                def _(sc, j=j, ldx_now=ldx_now, step=step):
                    if step == 0:
                        self.preload_set(sc, [AF.Silu])
                    W(sc, *ldx_now)
                    if step >= 2:
                        W(sc, "ve", ve_y2t[step - 2])
                    inc("se", sc.activation(t_sz[j][:], t_zr[j][:], AF.Silu))
                se_sz = V("se")

                @block.vector
                def _(ve, j=j, se_sz=se_sz, step=step):
                    W(ve, "se", se_sz)
                    if step >= 2:
                        W(ve, "te", te_op[step - 2])
                    inc("ve", ve.tensor_mul(t_y2[j][:], t_y1r[j][:], t_sz[j][:]))
                ve_y2t[step] = V("ve")

                @block.tensor
                def _(te, tb=tb, ic=ic, j=j, ldw_now=ldw_now, step=step,
                      se_sev=se_sev):
                    W(te, *ldw_now)
                    W(te, "ve", ve_y2t[step])
                    if ic == 0 and tb > 0:
                        W(te, "se", se_sev)
                    for dc in range(NDC):
                        i_ = te.matmul(p_s[dc][:],
                                       t_oslab[j][:, dc * 128:(dc + 1) * 128],
                                       t_y2[j][:],
                                       start=(ic == 0), stop=(ic == NIC - 1))
                    inc("te", i_)
                te_op[step] = V("te")

            @block.scalar
            def _(sc, tb=tb):
                W(sc, "te", te_op[tb * NIC + NIC - 1])
                if tb > 0:
                    W(sc, "ssv", V("ssv"))
                for dc in range(NDC):
                    inc("se", sc.copy(t_sev[dc][:], p_s[dc][:]))
            se_sev = V("se")

            @block.sync
            def _(sync, tb=tb, se_sev=se_sev):
                W(sync, "se", se_sev)
                for dc in range(NDC):
                    inc("ssv", sync.dma_start(
                        out=self.sp_s[dc][:, tb * 512:(tb + 1) * 512],
                        in_=t_sev[dc][:]), 16)
        self.st_s = ("ssv", V("ssv"))

        @block.vector
        def _(ve):
            W(ve, *self.st_s)
        g5.close()

        if "s" in self.taps:
            o = self.dout("dbg_s", (NDC, 128, T), BF16)

            @block.sync
            def _(sync):
                W(sync, *self.st_s)
                for dc in range(NDC):
                    inc("sdbg", sync.dma_start(out=o[dc], in_=self.sp_s[dc]), 16)

    def p6_ffn_prep(self):
        block, inc, V, W = self.block, self.inc, self.V, self.W
        sb, g67 = self.sb, self.g67
        self.t_tfmg = [sb(f"t_tfmg{i}", [128, 512], BF16, g67)
                       for i in range(NDC)]
        self.t_b1s = [sb(f"t_b1s{i}", [128, FF], BF16, g67) for i in range(2)]
        self.t_b3s = [sb(f"t_b3s{i}", [128, FF], BF16, g67) for i in range(2)]
        self.t_a2s = sb("t_a2s", [128, FF], BF16, g67)
        self.t_b2s = [sb(f"t_b2s{i}", [128, D], BF16, g67) for i in range(2)]
        self.t_u1 = [sb(f"t_u1{i}", [128, T], BF16, g67) for i in range(2)]
        self.t_u3 = [sb(f"t_u3{i}", [128, T], BF16, g67) for i in range(2)]
        self.t_wfmg = sb("t_wfmg", [NE, 512], BF16, g67)
        g6a = ExitStack()
        self.g6a = g6a
        self.t_wfm = sb("t_wfm", [NE, T], BF16, g6a)
        self.t_a1s = [sb(f"t_a1s{i}", [128, 1024], BF16, g6a) for i in range(2)]
        self.t_a3s = [sb(f"t_a3s{i}", [128, 1024], BF16, g6a) for i in range(2)]


        @block.gpsimd
        def _(gp):
            W(gp, "st", self.st_bc)
            for nm, v in self.spill_p2:
                W(gp, nm, v)
            for src, dst in [(self.a1TA, self.t_a1s[0]), (self.a1TB, self.t_a1s[1]),
                             (self.a3TA, self.t_a3s[0]), (self.a3TB, self.t_a3s[1])]:
                inc("lff", gp.dma_start(
                    out=dst[:].rearrange("p (i c) -> p i c", i=NDC),
                    in_=src.rearrange("(i p) c -> p i c", i=NDC, p=128)), 16)
            for src, dst in [(self.b1PA, self.t_b1s[0]), (self.b1PB, self.t_b1s[1]),
                             (self.b3PA, self.t_b3s[0]), (self.b3PB, self.t_b3s[1])]:
                inc("lff", gp.dma_start(out=dst[:], in_=src[:]), 16)
            inc("lff", gp.dma_start(
                out=self.t_a2s[:].rearrange("p (i c) -> p i c", i=FF // 128),
                in_=self.a2Td.rearrange("(i p) c -> p i c", i=FF // 128, p=128)), 16)
            inc("lff", gp.dma_start(out=self.t_b2s[0][:], in_=self.b2PA[:]), 16)
            inc("lff", gp.dma_start(out=self.t_b2s[1][:], in_=self.b2PB[:]), 16)
        gd_ffnw = V("lff")

        @block.vector
        def _(ve):
            W(ve, "lff", gd_ffnw)
            for ti, t in enumerate([self.t_b1s[0], self.t_b1s[1],
                                    self.t_b3s[0], self.t_b3s[1]]):
                for r in ([0, 32, 64, 96] if ti % 2 == 0 else [0, 32, 64]):
                    inc("ve", ve.tensor_scalar(t[r:r + NR, :], t[r:r + NR, :],
                                               -1.0, None, ALU.mult))
        self.ve_negb = V("ve")

        with ExitStack() as pp:
            p_u = self.ps("p_u", [128, 512], pp)
            first = True
            for tg in range(4):
                @block.sync
                def _(sync, tg=tg, first=first):
                    if first:
                        W(sync, "st", self.st_bc)
                        W(sync, "se", self.se_proj)
                        for nm, v in self.spill_p2:
                            W(sync, nm, v)
                    else:
                        W(sync, "te", V("te"))
                    for dc in range(NDC):
                        inc("ltf", sync.dma_start(
                            out=self.t_tfmg[dc][:],
                            in_=self.sp_tfm[dc][:, tg * 512:(tg + 1) * 512]),
                            16)
                ld_tf = V("ltf")
                for half in range(2):
                    for asrc, udst in [(self.t_a1s, self.t_u1),
                                       (self.t_a3s, self.t_u3)]:
                        @block.tensor
                        def _(te, half=half, tg=tg, asrc=asrc, first=first,
                              ld_tf=ld_tf):
                            W(te, "ltf", ld_tf)
                            if first:
                                W(te, "lff", gd_ffnw)
                                W(te, "se", self.se_proj)
                            else:
                                W(te, "se", V("se"))
                            for dc in range(NDC):
                                i_ = te.matmul(
                                    p_u[:],
                                    asrc[half][:, dc * 128:(dc + 1) * 128],
                                    self.t_tfmg[dc][:],
                                    start=(dc == 0), stop=(dc == NDC - 1))
                            inc("te", i_)
                        te_u = V("te")
                        first = False

                        @block.scalar
                        def _(sc, half=half, tg=tg, udst=udst, te_u=te_u):
                            W(sc, "te", te_u)
                            inc("se", sc.activation(
                                udst[half][:, tg * 512:(tg + 1) * 512], p_u[:],
                                AF.Copy, scale=self.t_twos[:, 0:1]))
            self.se_u = V("se")

        with ExitStack() as pp:
            p_wt = self.ps("p_wt", [NE, 128], pp)
            for tc in range(NTC):
                @block.tensor
                def _(te, tc=tc):
                    W(te, "se", V("se"))
                    i_ = te.transpose(p_wt[:],
                                      self.t_wtm[:, tc * NE:(tc + 1) * NE],
                                      self.t_eyef[:])
                    inc("te", i_)
                te_wt = V("te")

                @block.scalar
                def _(sc, tc=tc, te_wt=te_wt):
                    W(sc, "te", te_wt)
                    inc("se", sc.copy(
                        self.t_wfm[:, tc * 128:(tc + 1) * 128], p_wt[:]))
            self.se_wfm = V("se")

        @block.sync
        def _(sync):
            W(sync, "se", self.se_wfm)
            inc("swf", sync.dma_start(out=self.sp_wfm[:], in_=self.t_wfm[:]),
                16)
        self.st_wfm = V("swf")

        @block.tensor
        def _(te):
            W(te, "swf", self.st_wfm)
        self.g6a.close()

    def p7_ffn_main(self):
        block, inc, V, W = self.block, self.inc, self.V, self.W
        sb, g67 = self.sb, self.g67
        t_w1s = [sb(f"t_w1s{i}", [128, NDC * 128], BF16, g67) for i in range(2)]
        t_w3s = [sb(f"t_w3s{i}", [128, NDC * 128], BF16, g67) for i in range(2)]
        t_w2s = [sb(f"t_w2s{i}", [128, FF // 4], BF16, g67) for i in range(2)]
        t_s1 = [sb(f"t_s1{i}", [128, 512], BF16, g67) for i in range(2)]
        t_t1 = [sb("t_t1", [128, 512], BF16, g67)] * 2
        t_pw = [sb(f"t_pw{i}", [128, 512], BF16, g67) for i in range(2)]
        t_mfm = [sb(f"t_mfm{i}", [128, 512], BF16, g67) for i in range(32)]
        t_wb = [sb(f"t_wb{i}", [128, 512], BF16, g67) for i in range(NE)]
        t_u2w = [sb(f"t_u2w{i}", [128, 512], BF16, g67) for i in range(2)]
        t_g = [sb(f"t_g{i}", [128, 512], BF16, g67) for i in range(2)]

        @block.vector
        def _(ve):
            inc("ve", ve.memset(t_u2w[0][:], 0.0))
            inc("ve", ve.memset(t_u2w[1][:], 0.0))
        ve_u2z = V("ve")

        self.p4_init()
        self.p4_xcr(0)
        self.p4_dt(0)
        self.p4_exps(0)

        gd_w = {}
        te_base1 = {}
        te_base3 = {}
        te_d1 = {}
        te_d3 = {}
        te_u2 = {}
        se_s1 = {}
        ve_pw = {}
        gp_m = {}
        se_w2_last = {}
        te_w2_last = {}
        ld_tfmg = {}

        for g in range(NTG):
            @block.sync
            def _(sync, g=g):
                if g == 0:
                    W(sync, "te", V("te"))
                    W(sync, "swf", self.st_wfm)
                else:
                    W(sync, "te", te_base3[(g - 1, 31)])
                for dc in range(NDC):
                    inc("ltf", sync.dma_start(
                        out=self.t_tfmg[dc][:],
                        in_=self.sp_tfm[dc][:, g * 512:(g + 1) * 512]), 16)
                inc("ltf", sync.dma_start(
                    out=self.t_wfmg[:],
                    in_=self.sp_wfm[:, g * 512:(g + 1) * 512]), 16)
            ld_tfmg[g] = V("ltf")

            with ExitStack() as ppw:
                p_wb = [self.ps(f"p_wb_{g}_{i}", [128, 512], ppw) for i in range(2)]
                for e in range(NE):
                    @block.tensor
                    def _(te, g=g, e=e):
                        if e == 0:
                            W(te, "se", self.se_wfm)
                            if g == 0:
                                W(te, "se", self.se_u)
                            else:
                                W(te, "se", se_w2_last[g - 1])
                                W(te, "ve", gp_m[(g - 1, 31, NE - 1)])
                        if e >= 2:
                            W(te, "se", se_wb[e - 2])
                        if e == 0:
                            W(te, "ltf", ld_tfmg[g])
                        i_ = te.matmul(p_wb[e % 2][:],
                                       self.t_wbsel[:, e * 128:(e + 1) * 128],
                                       self.t_wfmg[:],
                                       start=True, stop=True)
                        inc("te", i_)
                    te_wb = V("te")
                    if e == 0:
                        se_wb = {}

                    @block.scalar
                    def _(sc, g=g, e=e, te_wb=te_wb):
                        W(sc, "te", te_wb)
                        inc("se", sc.copy(t_wb[e][:], p_wb[e % 2][:]))
                    se_wb[e] = V("se")
            se_wb_all = V("se")

            with ExitStack() as ppA:
                p_h1 = [self.ps(f"p_h1_{g}_{i}", [128, 512], ppA)
                        for i in range(2)]
                p_h3 = [self.ps(f"p_h3_{g}_{i}", [128, 512], ppA)
                        for i in range(2)]
                p_U2 = [self.ps(f"p_U2_{g}_{i}", [128, 512], ppA)
                        for i in range(2)]

                def emit_slab(nf, g=g):
                    wb = nf % 2

                    @block.gpsimd
                    def _(gp, g=g, nf=nf, wb=wb):
                        if g == 0 and nf <= 1:
                            W(gp, "swf", self.st_wfm)
                        if nf >= 2:
                            W(gp, "te", te_base3[(g, nf - 2)])
                        elif g >= 1:
                            W(gp, "te", te_base3[(g - 1, 30 + nf)])
                        for wsrc, wdst in [(self.w1T, t_w1s[wb]),
                                           (self.w3T, t_w3s[wb])]:
                            inc(f"gw{wb}", gp.dma_start(
                                out=wdst[:].rearrange("p (i c) -> p i c", i=NDC),
                                in_=wsrc.rearrange("(i p) c -> p i c",
                                                   i=NDC, p=128)[
                                    :, :, nf * 128:(nf + 1) * 128]), 16)
                    gd_w[(g, nf)] = (f"gw{wb}", V(f"gw{wb}"))

                def emit_base(nfsl, br, dlo, dhi, g=g):
                    nwb = nfsl % 2
                    fb = nfsl % 2
                    wsl = t_w1s if br == 0 else t_w3s
                    bsl = self.t_b1s if br == 0 else self.t_b3s
                    usl = self.t_u1 if br == 0 else self.t_u3
                    ph = (p_h1 if br == 0 else p_h3)[fb]

                    @block.tensor
                    def _(te, g=g, nfsl=nfsl, br=br, dlo=dlo, dhi=dhi,
                          nwb=nwb, wsl=wsl, bsl=bsl,
                          usl=usl, ph=ph):
                        if dlo == 0:
                            W(te, *gd_w[(g, nfsl)])
                            if nfsl == 0:
                                W(te, "se", se_wb_all)
                                W(te, "ltf", ld_tfmg[g])
                                if g == 0 and br == 0:
                                    W(te, "ve", self.ve_negb)
                            if nfsl >= 2:
                                if br == 0:
                                    W(te, "se", se_s1[(g, nfsl - 2, NE - 1)])
                                else:
                                    W(te, "ve", ve_pw[(g, nfsl - 2, NE - 1)])
                            elif g >= 1:
                                if br == 0:
                                    W(te, "se",
                                      se_s1[(g - 1, 30 + nfsl, NE - 1)])
                                else:
                                    W(te, "ve",
                                      ve_pw[(g - 1, 30 + nfsl, NE - 1)])
                        for dc in range(dlo, dhi):
                            i_ = te.matmul(
                                ph[:],
                                wsl[nwb][:, dc * 128:(dc + 1) * 128],
                                self.t_tfmg[dc][:],
                                start=(dc == 0), stop=False)
                        if dhi == NDC:
                            i_ = te.matmul(
                                ph[:],
                                bsl[1][96:112, nfsl * 128:(nfsl + 1) * 128],
                                usl[1][96:112, g * 512:(g + 1) * 512],
                                start=False, stop=True,
                                tile_position=(96, 0))
                        inc("te", i_)
                    if dhi == NDC:
                        (te_base1 if br == 0 else te_base3)[(g, nfsl)] = \
                            V("te")

                def emit_d1(fsl, e, g=g):
                    fb = fsl % 2
                    ab = 0 if e <= 4 else 1
                    rb = 32 * ((e - 1) % 4)
                    kwr = dict(tile_position=(96, 0)) if rb == 96 else {}

                    @block.tensor
                    def _(te, g=g, fsl=fsl, e=e, fb=fb, ab=ab, rb=rb, kwr=kwr):
                        W(te, "se", se_s1[(g, fsl, e - 1)])
                        i_ = te.matmul(
                            p_h1[fb][:],
                            self.t_b1s[ab][rb:rb + 32,
                                           fsl * 128:(fsl + 1) * 128],
                            self.t_u1[ab][rb:rb + 32,
                                          g * 512:(g + 1) * 512],
                            start=False, stop=True,
                            skip_group_check=True, **kwr)
                        inc("te", i_)
                    te_d1[(g, fsl, e)] = V("te")

                def emit_d3(fsl, e, g=g):
                    fb = fsl % 2
                    ab = 0 if e <= 4 else 1
                    rb = 32 * ((e - 1) % 4)
                    kwr = dict(tile_position=(96, 0)) if rb == 96 else {}

                    @block.tensor
                    def _(te, g=g, fsl=fsl, e=e, fb=fb, ab=ab, rb=rb, kwr=kwr):
                        W(te, "ve", ve_pw[(g, fsl, e - 1)])
                        i_ = te.matmul(
                            p_h3[fb][:],
                            self.t_b3s[ab][rb:rb + 32,
                                           fsl * 128:(fsl + 1) * 128],
                            self.t_u3[ab][rb:rb + 32,
                                          g * 512:(g + 1) * 512],
                            start=False, stop=True,
                            skip_group_check=True, **kwr)
                        inc("te", i_)
                    te_d3[(g, fsl, e)] = V("te")

                def emit_u2(fsl, e2, g=g):
                    eb2 = e2 % 2
                    half2 = e2 // 4
                    j2 = e2 % 4
                    lo2 = 32 * j2

                    @block.tensor
                    def _(te, g=g, fsl=fsl, e2=e2, eb2=eb2, half2=half2,
                          j2=j2, lo2=lo2):
                        W(te, "ve", ve_pw[(g, fsl, e2)])
                        kwc = (dict(tile_position=(0, 96))
                               if j2 == 3 else {})
                        a2sl = self.t_a2s[:, fsl * 128 + NR * e2:
                                          fsl * 128 + NR * (e2 + 1)]
                        i_ = te.matmul(
                            p_U2[half2][lo2:lo2 + NR, :], a2sl,
                            t_pw[eb2][:],
                            start=(fsl == 0), stop=(fsl == 31),
                            skip_group_check=True, **kwc)
                        inc("te", i_)
                    te_u2[(g, fsl, e2)] = V("te")

                def emit_silu(fsl, e, g=g):
                    fb = fsl % 2
                    eb = e % 2

                    @block.scalar
                    def _(sc, g=g, fsl=fsl, e=e, fb=fb, eb=eb):
                        self.preload_set(sc, [AF.Silu])
                        if e == 0:
                            W(sc, "te", te_base1[(g, fsl)])
                        else:
                            W(sc, "te", te_d1[(g, fsl, e)])
                        key = ((g, fsl, e - 2) if e >= 2 else
                               (g, fsl - 1, e + 6) if fsl >= 1 else
                               (g - 1, 31, e + 6) if g >= 1 else None)
                        if key is not None:
                            W(sc, "ve", ve_pw[key])
                        inc("se", sc.activation(t_s1[eb][:], p_h1[fb][:],
                                                AF.Silu))
                    se_s1[(g, fsl, e)] = V("se")

                def emit_pw(fsl, e, g=g):
                    fb = fsl % 2
                    eb = e % 2

                    @block.vector
                    def _(ve, g=g, fsl=fsl, e=e, fb=fb, eb=eb):
                        W(ve, "se", se_s1[(g, fsl, e)])
                        key = ((g, fsl, e - 2) if e >= 2 else
                               (g, fsl - 1, e + 6) if fsl >= 1 else
                               (g - 1, 31, e + 6) if g >= 1 else None)
                        if key is not None:
                            W(ve, "te", te_u2[key])
                        inc("ve", ve.tensor_mul(t_t1[eb][:], t_s1[eb][:],
                                                t_wb[e][:]))
                        if e == 0:
                            W(ve, "te", te_base3[(g, fsl)])
                        else:
                            W(ve, "te", te_d3[(g, fsl, e)])
                        W(ve, "ve")
                        inc("ve", ve.tensor_mul(t_pw[eb][:], t_t1[eb][:],
                                                p_h3[fb][:]))
                    ve_pw[(g, fsl, e)] = V("ve")

                def emit_macc(fsl, e, g=g):
                    eb = e % 2

                    @block.vector
                    def _(ve, g=g, fsl=fsl, e=e, eb=eb):
                        if fsl == 0 and e == 0 and g >= 1:
                            W(ve, "te", te_w2_last[g - 1])
                        W(ve, "ve")
                        if e == 0:
                            inc("ve", ve.tensor_copy(t_mfm[fsl][:],
                                                     t_pw[eb][:]))
                        else:
                            inc("ve", ve.tensor_add(t_mfm[fsl][:],
                                                    t_mfm[fsl][:],
                                                    t_pw[eb][:]))
                    gp_m[(g, fsl, e)] = V("ve")

                emit_slab(0)
                emit_slab(1)
                emit_base(0, 0, 0, NDC)
                emit_base(0, 1, 0, NDC)
                for fsl in range(32):
                    if fsl + 2 <= 31:
                        emit_slab(fsl + 2)
                    for e in range(NE):
                        if e >= 1:
                            emit_d1(fsl, e)
                            emit_d3(fsl, e)
                            emit_u2(fsl, e - 1)
                        if fsl < 31:
                            if e == 3:
                                emit_base(fsl + 1, 0, 0, 4)
                            elif e == 4:
                                emit_base(fsl + 1, 0, 4, NDC)
                            elif e == 5:
                                emit_base(fsl + 1, 1, 0, 4)
                            elif e == 6:
                                emit_base(fsl + 1, 1, 4, NDC)
                        emit_silu(fsl, e)
                        emit_pw(fsl, e)
                        emit_macc(fsl, e)
                        self.p4_advance(0.8125)
                    emit_u2(fsl, NE - 1)

                @block.scalar
                def _(sc, g=g):
                    W(sc, "te", te_u2[(g, 31, NE - 1)])
                    if g == 0:
                        W(sc, "ve", ve_u2z)
                    for half in range(2):
                        for j in range(4):
                            lo = 32 * j
                            inc("se", sc.activation(
                                t_u2w[half][lo:lo + NR, :],
                                p_U2[half][lo:lo + NR, :], AF.Copy,
                                scale=self.t_twos[0:NR, 0:1]))
                se_u2w = V("se")

            with ExitStack() as ppB:
                p_o = [self.ps(f"p_o_{g}_{i}", [128, 512], ppB) for i in range(2)]
                for dcO in range(NDC):
                    ob = dcO % 2
                    if dcO == 0:
                        te_o = {}
                        te_o2 = {}

                    for fh in range(4):
                        @block.gpsimd
                        def _(gp, g=g, dcO=dcO, fh=fh):
                            step2 = dcO * 4 + fh
                            if step2 >= 2:
                                W(gp, "te", te_o2[step2 - 2])
                            elif g >= 1:
                                W(gp, "te", te_o2_prev[4 * NDC - 2 + step2])
                            inc(f"g2w{fh % 2}", gp.dma_start(
                                out=t_w2s[fh % 2][:].rearrange(
                                    "p (i c) -> p i c", i=8),
                                in_=self.w2T.rearrange(
                                    "(i p) c -> p i c", i=32, p=128)[
                                    :, fh * 8:(fh + 1) * 8,
                                    dcO * 128:(dcO + 1) * 128]), 16)
                        gd_w2f = (f"g2w{fh % 2}", V(f"g2w{fh % 2}"))

                        @block.tensor
                        def _(te, g=g, dcO=dcO, ob=ob, fh=fh, gd_w2f=gd_w2f):
                            W(te, *gd_w2f)
                            if dcO == 0 and fh == 0:
                                W(te, "se", se_u2w)
                                W(te, "ve", gp_m[(g, 31, NE - 1)])
                            if fh == 0 and dcO >= 2:
                                W(te, "se", se_g[dcO - 2])
                            for fl in range(8):
                                i_ = te.matmul(
                                    p_o[ob][:],
                                    t_w2s[fh % 2][:, fl * 128:(fl + 1) * 128],
                                    t_mfm[fh * 8 + fl][:],
                                    start=(fh == 0 and fl == 0), stop=False)
                            if fh == 3:
                                te.matmul(p_o[ob][:],
                                          self.t_b2s[0][:,
                                                       dcO * 128:(dcO + 1) * 128],
                                          t_u2w[0][:], start=False, stop=False)
                                i_ = te.matmul(
                                    p_o[ob][:],
                                    self.t_b2s[1][:, dcO * 128:(dcO + 1) * 128],
                                    t_u2w[1][:], start=False, stop=True)
                            inc("te", i_)
                        te_o2[dcO * 4 + fh] = V("te")
                        self.p4_advance(1.5)
                    te_o[dcO] = V("te")
                    if dcO == 0:
                        se_g = {}

                    @block.scalar
                    def _(sc, g=g, dcO=dcO, ob=ob):
                        self.preload_set(sc, [AF.Silu])
                        W(sc, "te", te_o[dcO])
                        if dcO >= 2:
                            W(sc, f"sg{ob}", sg_o[dcO - 2][1])
                        elif g >= 1:
                            W(sc, f"sg{ob}", sg_o_prev[NDC - 2 + dcO][1])
                        inc("se", sc.activation(t_g[ob][:], p_o[ob][:], AF.Silu))
                    se_g[dcO] = V("se")
                    if dcO == NDC - 1:
                        se_w2_last[g] = se_g[dcO]
                    if dcO == 0:
                        sg_o = {}

                    if g == 0 and "gout" in self.taps:
                        o = self.dbg.get("dbg_gout")
                        if o is None:
                            o = self.dout("dbg_gout", (NDC, 128, 512))

                        @block.sync
                        def _(sync, dcO=dcO, ob=ob):
                            W(sync, "se", se_g[dcO])
                            inc("sdbg", sync.dma_start(out=o[dcO],
                                                       in_=t_g[ob][:]), 16)
                        sdbg_g = V("sdbg")

                        @block.vector
                        def _(ve, sdbg_g=sdbg_g):
                            W(ve, "sdbg", sdbg_g)

                    @block.sync
                    def _(sync, g=g, dcO=dcO, ob=ob):
                        W(sync, "se", se_g[dcO])
                        inc(f"sg{ob}", sync.dma_start(
                            out=self.sp_g[g][dcO], in_=t_g[ob][:]), 16)
                    sg_o[dcO] = (f"sg{ob}", V(f"sg{ob}"))
                te_w2_last[g] = te_o[NDC - 1]
                te_o_prev = dict(te_o)
                te_o2_prev = dict(te_o2)
                sg_o_prev = dict(sg_o)
            if g == 0 and "mfm" in self.taps:
                o = self.dout("dbg_mfm", (32, 128, 512), BF16)

                @block.sync
                def _(sync):
                    W(sync, "ve", gp_m[(0, 31, NE - 1)])
                    for i in range(32):
                        inc("sdbg", sync.dma_start(out=o[i], in_=t_mfm[i][:]), 16)
                sdbg_m = V("sdbg")

                @block.tensor
                def _(te):
                    W(te, "sdbg", sdbg_m)
        self.p4_close()

    def p8_resid(self):
        block, inc, V, W = self.block, self.inc, self.V, self.W
        sb = self.sb
        g8 = ExitStack()
        t_gq = [sb(f"t_gq{i}", [128, 512], BF16, g8) for i in range(2)]
        t_sq = [sb(f"t_sq{i}", [128, 512], BF16, g8) for i in range(2)]
        t_xq = [sb(f"t_xq{i}", [128, 512], F32, g8) for i in range(2)]
        t_oq = [sb(f"t_oq{i}", [128, 512], F32, g8) for i in range(2)]

        ve_o = {}
        for step in range(NTG * NDC):
            g, dcO = divmod(step, NDC)
            j = step % 2

            @block.sync
            def _(sync, g=g, dcO=dcO, j=j, step=step):
                if step < 2:
                    W(sync, *self.st_s)
                    W(sync, "sg0", V("sg0"))
                    W(sync, "sg1", V("sg1"))
                    W(sync, "te", V("te"))
                    W(sync, "se", V("se"))
                    W(sync, "ve", V("ve"))
                else:
                    W(sync, "ve", ve_o[step - 2])
                inc(f"lrs{j}", sync.dma_start(out=t_gq[j][:],
                                              in_=self.sp_g[g][dcO]), 16)
                inc(f"lrs{j}", sync.dma_start(
                    out=t_sq[j][:],
                    in_=self.sp_s[dcO][:, g * 512:(g + 1) * 512]), 16)
                inc(f"lrs{j}", sync.dma_start(
                    out=t_xq[j][:],
                    in_=self.x_fm[dcO * 128:(dcO + 1) * 128,
                                  g * 512:(g + 1) * 512]), 16)
            ld_now = (f"lrs{j}", V(f"lrs{j}"))

            @block.vector
            def _(ve, j=j, step=step, ld_now=ld_now):
                W(ve, *ld_now)
                if step >= 2:
                    W(ve, f"out{j}", out_done[step - 2][1])
                inc("ve", ve.tensor_mul(t_oq[j][:], t_sq[j][:], t_gq[j][:]))
                W(ve, "ve")
                inc("ve", ve.tensor_add(t_oq[j][:], t_oq[j][:], t_xq[j][:]))
            ve_o[step] = V("ve")
            if step == 0:
                out_done = {}

            @block.sync
            def _(sync, g=g, dcO=dcO, j=j, step=step):
                W(sync, "ve", ve_o[step])
                inc(f"out{j}", sync.dma_start(
                    out=self.out_fm[dcO * 128:(dcO + 1) * 128,
                                    g * 512:(g + 1) * 512],
                    in_=t_oq[j][:]), 16)
            out_done[step] = (f"out{j}", V(f"out{j}"))

        @block.vector
        def _(ve):
            W(ve, "out0", V("out0"))
            W(ve, "out1", V("out1"))
        g8.close()

    def finish(self):
        block, V, W = self.block, self.V, self.W

        @block.sync
        def _(sync):
            for nm in list(self.sems.keys()):
                if nm[0] in "so" or nm.startswith("out"):
                    W(sync, nm, V(nm))



def _pack_inputs(inp):
    f32 = np.float32
    x = np.asarray(inp["L0_attn_x"])
    ipw = np.asarray(inp["L0_attn_in_proj_w"])
    opw = np.asarray(inp["L0_attn_out_proj_w"])
    cw = np.asarray(inp["L0_attn_conv_w"])
    cb = np.asarray(inp["L0_attn_conv_b"])
    dtw = np.asarray(inp["L0_attn_dt_proj_w"])
    dtb = np.asarray(inp["L0_attn_dt_proj_b"])
    xpw = np.asarray(inp["L0_attn_x_proj_w"])
    alog = np.asarray(inp["L0_attn_A_log"])
    dskip = np.asarray(inp["L0_attn_D_skip"])
    nw = np.asarray(inp["L0_ffn_norm_w"])
    w1 = np.asarray(inp["L0_ffn_w1"])
    w3 = np.asarray(inp["L0_ffn_w3"])
    w2 = np.asarray(inp["L0_ffn_w2"])
    gw = np.asarray(inp["L0_ffn_gate_w"])
    a1 = np.asarray(inp["L0_ffn_a1"])
    b1 = np.asarray(inp["L0_ffn_b1"])
    a3 = np.asarray(inp["L0_ffn_a3"])
    b3 = np.asarray(inp["L0_ffn_b3"])
    a2 = np.asarray(inp["L0_ffn_a2"])
    b2 = np.asarray(inp["L0_ffn_b2"])

    c = np.ascontiguousarray
    sh = {}
    sh["ipT"] = c(ipw.T)
    sh["opT"] = c(opw.T)
    xpT = xpw.T
    xpTp = np.zeros((DI, 112), dtype=xpT.dtype)
    xpTp[:, 0:64] = xpT[:, 0:64]
    xpTp[:, 64:80] = xpT[:, 64:80]
    xpTp[:, 96:112] = xpT[:, 80:96]
    sh["xpTp"] = xpTp
    sh["dtT"] = c(dtw.T)
    sh["A_pack"] = c(alog.reshape(NIC, 128, NST).transpose(1, 0, 2)
                     .reshape(128, NIC * NST))
    sh["dcol"] = c(dskip.reshape(NIC, 128).T.astype(f32))
    sh["cbcol"] = c(cb.reshape(NIC, 128).T.astype(f32))
    sh["dtbcol"] = c(dtb.reshape(NIC, 128).T.astype(f32))
    sh["convw"] = c(cw.reshape(DI, DCONV).reshape(NIC, 128, DCONV)
                    .transpose(1, 0, 2).reshape(128, NIC * DCONV).astype(f32))
    sh["nwp"] = c(nw.reshape(NDC, 128).T.astype(f32))
    sh["gwT"] = c(gw.T.astype(f32))
    sh["w1T"] = c(w1.T.astype(f32))
    sh["w3T"] = c(w3.T.astype(f32))
    sh["w2T"] = c(w2.T.astype(f32))

    def pack_aT(a):
        outA = np.zeros((D, 128), f32)
        outB = np.zeros((D, 128), f32)
        for i, e in enumerate([0, 1, 1, 2, 2, 3, 3, 4]):
            outA[:, 16 * i:16 * i + NR] = a[e].T
        for i, e in enumerate([4, 5, 5, 6, 6, 7]):
            outB[:, 16 * i:16 * i + NR] = a[e].T
        outB[:, 96:112] = a[0].T
        return outA, outB

    sh["a1TA"], sh["a1TB"] = pack_aT(a1.astype(f32))
    sh["a3TA"], sh["a3TB"] = pack_aT(a3.astype(f32))

    def pack_bP(b):
        outA = np.zeros((128, FF), f32)
        outB = np.zeros((128, FF), f32)
        for k in range(4):
            outA[32 * k:32 * k + NR, :] = b[k].T
            outA[32 * k + NR:32 * k + 32, :] = b[k + 1].T
        for k in range(3):
            outB[32 * k:32 * k + NR, :] = b[k + 4].T
            outB[32 * k + NR:32 * k + 32, :] = b[k + 5].T
        outB[96:112, :] = b[0].T
        return outA, outB

    sh["b1PA"], sh["b1PB"] = pack_bP(b1.astype(f32))
    sh["b3PA"], sh["b3PB"] = pack_bP(b3.astype(f32))
    sh["a2T"] = c(a2.reshape(NE * NR, FF).T.astype(f32))

    def pack_b2(bq):
        outA = np.zeros((128, D), f32)
        outB = np.zeros((128, D), f32)
        for j in range(4):
            outA[32 * j:32 * j + NR, :] = bq[j].T
            outB[32 * j:32 * j + NR, :] = bq[j + 4].T
        return outA, outB

    sh["b2PA"], sh["b2PB"] = pack_b2(b2.astype(f32))

    selT = np.zeros((NST, NST * 128), bfdt)
    for n in range(NST):
        selT[n, n * 128:(n + 1) * 128] = 1.0
    sh["selT"] = selT
    wbsel = np.zeros((NE, NE * 128), bfdt)
    for e in range(NE):
        wbsel[e, e * 128:(e + 1) * 128] = 1.0
    sh["wbsel"] = wbsel
    sh["eye_bf"] = np.eye(128, dtype=bfdt)
    sh["eye_f"] = np.eye(128, dtype=f32)
    sh["ones_f"] = np.ones((128, 1), f32)
    sh["ones_b"] = np.ones((128, 1), bfdt)
    sh["onesc_f"] = np.ones((1, 128), f32)
    sh["onesc_b"] = np.ones((1, 128), bfdt)
    sh["epsv"] = np.array([[EPS_LN, EPS_RMS, 0.0, 0.0]], f32)
    sh["twos_f"] = np.full((128, 1), 2.0, f32)

    maps = []
    for b in range(8):
        m = dict(sh)
        m["x_fm"] = c(x[b].T.astype(f32))
        maps.append(m)
    return maps


def kernel(**inputs):
    maps = _pack_inputs(inputs)
    nc, ins, dbg = Build(DEBUG_TAPS).build()
    res = run_bass_kernel_spmd(nc, maps, core_ids=list(range(8)))
    out = np.stack([np.asarray(res.results[b]["out_fm"], np.float32).T
                    for b in range(8)])
    return out


# version 21 (final)
# speedup vs baseline: 1.0168x; 1.0168x over previous
import numpy as np
import ml_dtypes
from contextlib import ExitStack

import concourse.bass as bass
import concourse.mybir as mybir
from concourse.bass_utils import run_bass_kernel_spmd

F32 = mybir.dt.float32
BF16 = mybir.dt.bfloat16
AF = mybir.ActivationFunctionType
ALU = mybir.AluOpType
AX = mybir.AxisListType
bfdt = ml_dtypes.bfloat16

T = 2048
D = 1024
DI = 2048
NST = 16
DCONV = 4
DTR = 64
FF = 4096
NE = 8
NR = 16
NDC = D // 128
NIC = DI // 128
NTC = T // 128
NTG = 4
NFC = FF // 512
NQ = 4
QW = T // NQ
EPS_LN = 1e-5
EPS_RMS = 1e-6

MAX_PHASE = 7
DEBUG_TAPS = ()
SCAN_GP_N = {2, 5, 8, 11, 13, 15}


class Build:
    def __init__(self, debug_taps=()):
        self.taps = set(debug_taps)
        self.nc = bass.Bass("TRN2", target_bir_lowering=False, debug=False)
        self.ins = {}
        self.dbg = {}
        self.es = ExitStack()
        self.sems = {}

    def din(self, name, shape, dt):
        ap = self.nc.dram_tensor(name, list(shape), dt, kind="ExternalInput").ap()
        self.ins[name] = ap
        return ap

    def dout(self, name, shape, dt=F32):
        ap = self.nc.dram_tensor(name, list(shape), dt, kind="ExternalOutput").ap()
        self.dbg[name] = ap
        return ap

    def dram(self, name, shape, dt):
        return self.nc.dram_tensor(name, list(shape), dt, kind="Internal").ap()

    def sb(self, name, shape, dt, stack=None):
        return (stack or self.es).enter_context(
            self.nc.sbuf_tensor(name, shape, dt))

    def ps(self, name, shape, stack=None):
        return (stack or self.es).enter_context(
            self.nc.psum_tensor(name, shape, F32))

    def S(self, name):
        if name not in self.sems:
            self.sems[name] = [self.es.enter_context(self.nc.semaphore(name)), 0]
        return self.sems[name]

    def inc(self, name, inst, amt=1):
        s = self.S(name)
        s[1] += amt
        inst.then_inc(s[0], amt)
        return s[1]

    def V(self, name):
        return self.S(name)[1]

    def W(self, eng, name, v=None):
        s = self.S(name)
        eng.wait_ge(s[0], s[1] if v is None else v)

    def preload_set(self, sc, funcs):
        from concourse.hw_specs import get_activation_tables
        for idx, fs in enumerate(get_activation_tables(self.nc.m.arch).values()):
            if all(f in fs for f in funcs):
                if getattr(self, "_cur_set", None) == idx:
                    return None
                self._cur_set = idx
                return sc.add_instruction(mybir.InstLoadActFuncSet(
                    name=self.nc.get_next_instruction_name(),
                    act_func_set_id=idx, ins=[], outs=[]))
        raise ValueError(f"no table set contains {funcs}")

    def build(self):
        nc = self.nc
        with self.es:
            self.block = self.es.enter_context(nc.Block())
            self.declare_io()
            self.p0_consts()
            self.p1_ln_gate()
            if MAX_PHASE >= 2:
                self.p2_inproj_conv()
            if MAX_PHASE >= 7:
                self.g67 = ExitStack()
                self.p6_ffn_prep()
                self.p7_ffn_main()
                self.g67.close()
            elif MAX_PHASE >= 4:
                self.p4_init()
                for k in range(NQ * NIC):
                    self.p4_piece(k)
                self.p4_close()
            self.gE.close()
            self.gA.close()
            if MAX_PHASE >= 5:
                self.p5_outproj()
            if MAX_PHASE >= 7:
                self.p8_resid()
            self.finish()
            for st in [getattr(self, "g67", None), self.g4, self.gE,
                       self.gB, self.gA]:
                if st is not None:
                    try:
                        st.close()
                    except Exception:
                        pass
        return nc, self.ins, self.dbg

    def declare_io(self):
        d = self.din
        self.x_fm = d("x_fm", (D, T), F32)
        self.ipT = d("ipT", (D, 2 * DI), BF16)
        self.opT = d("opT", (DI, D), BF16)
        self.xpTp = d("xpTp", (DI, 112), BF16)
        self.dtTd = d("dtT", (DTR, DI), BF16)
        self.A_pack = d("A_pack", (128, NIC * NST), BF16)
        self.dcold = d("dcol", (128, NIC), F32)
        self.cbcold = d("cbcol", (128, NIC), F32)
        self.dtbcold = d("dtbcol", (128, NIC), F32)
        self.convwd = d("convw", (128, NIC * DCONV), F32)
        self.nwpd = d("nwp", (128, NDC), F32)
        self.gwTd = d("gwT", (D, NE), F32)
        self.w1T = d("w1T", (D, FF), F32)
        self.w3T = d("w3T", (D, FF), F32)
        self.w2T = d("w2T", (FF, D), F32)
        self.a1TA = d("a1TA", (D, 128), F32)
        self.a1TB = d("a1TB", (D, 128), F32)
        self.a3TA = d("a3TA", (D, 128), F32)
        self.a3TB = d("a3TB", (D, 128), F32)
        self.b1PA = d("b1PA", (128, FF), F32)
        self.b1PB = d("b1PB", (128, FF), F32)
        self.b3PA = d("b3PA", (128, FF), F32)
        self.b3PB = d("b3PB", (128, FF), F32)
        self.a2Td = d("a2T", (FF, 128), F32)
        self.b2PA = d("b2PA", (128, D), F32)
        self.b2PB = d("b2PB", (128, D), F32)
        self.selTd = d("selT", (NST, NST * 128), BF16)
        self.wbseld = d("wbsel", (NE, NE * 128), BF16)
        self.eye_bf = d("eye_bf", (128, 128), BF16)
        self.eye_f = d("eye_f", (128, 128), F32)
        self.ones_f = d("ones_f", (128, 1), F32)
        self.ones_b = d("ones_b", (128, 1), BF16)
        self.onesc_f = d("onesc_f", (1, 128), F32)
        self.onesc_b = d("onesc_b", (1, 128), BF16)
        self.epsd = d("epsv", (1, 4), F32)
        self.twosd = d("twos_f", (128, 1), F32)

        self.sp_xconv = self.dram("sp_xconv", [NIC, 128, T], BF16)
        self.sp_z = self.dram("sp_z", [NIC, 128, T], BF16)
        self.sp_y1 = self.dram("sp_y1", [NIC, 128, T], BF16)
        self.sp_tfm = self.dram("sp_tfm", [NDC, 128, T], BF16)
        self.sp_s = self.dram("sp_s", [NDC, 128, T], BF16)
        self.out_fm = self.nc.dram_tensor("out_fm", [D, T], F32,
                                          kind="ExternalOutput").ap()

        self.sp_bc = self.dram("sp_bc", [2, NST, T], BF16)
        self.sp_g = self.dram("sp_g", [NTG, NDC, 128, 512], BF16)
        self.sp_wfm = self.dram("sp_wfm", [NE, T], BF16)

        self.gA = ExitStack()
        self.gB = ExitStack()
        self.gE = ExitStack()
        self.g4 = ExitStack()

    def p0_consts(self):
        block, inc, V, W = self.block, self.inc, self.V, self.W
        sb = self.sb
        self.t_eyeb = sb("t_eyeb", [128, 128], BF16)
        self.t_eyef = sb("t_eyef", [128, 128], F32)
        self.t_onesf = sb("t_onesf", [128, 1], F32)
        self.t_onesb = sb("t_onesb", [128, 1], BF16)
        self.t_onescf = sb("t_onescf", [1, 128], F32)
        self.t_onescb = sb("t_onescb", [1, 128], BF16)
        self.t_nwp = sb("t_nwp", [128, NDC], F32)
        self.t_Af = sb("t_Af", [128, NIC * NST], F32)
        self.t_dcol = sb("t_dcol", [128, NIC], F32)
        self.t_cbcol = sb("t_cbcol", [128, NIC], F32)
        self.t_dtbcol = sb("t_dtbcol", [128, NIC], F32)
        self.t_convw = sb("t_convw", [128, NIC * DCONV], F32)
        self.t_dtT = sb("t_dtT", [DTR, DI], BF16)
        self.t_wtm = sb("t_wtm", [128, NTC * NE], F32)
        self.t_wbsel = sb("t_wbsel", [NE, NE * 128], BF16)
        self.t_gw = sb("t_gw", [128, NDC * NE], F32)
        self.t_eps = sb("t_eps", [1, 4], F32)
        self.t_twos = sb("t_twos", [128, 1], F32)
        self.t_dtraw = sb("t_dtraw", [DTR, T], BF16, self.gA)
        t_Apack = sb("t_Apack", [128, NIC * NST], BF16, self.gA)
        t_gwraw = sb("t_gwraw", [128, NDC * NE], F32, self.gB)
        self.t_xp = sb("t_xp", [128, NIC * 112], BF16, self.gB)
        self.t_sel = sb("t_sel", [NST, NST * 128], BF16, self.gB)

        @block.sync
        def _(sync):
            for src, dst in [(self.eye_bf, self.t_eyeb), (self.eye_f, self.t_eyef),
                             (self.ones_f, self.t_onesf), (self.ones_b, self.t_onesb),
                             (self.onesc_f, self.t_onescf),
                             (self.onesc_b, self.t_onescb),
                             (self.nwpd, self.t_nwp),
                             (self.A_pack, t_Apack), (self.dcold, self.t_dcol),
                             (self.cbcold, self.t_cbcol),
                             (self.dtbcold, self.t_dtbcol),
                             (self.convwd, self.t_convw), (self.dtTd, self.t_dtT),
                             (self.selTd, self.t_sel),
                             (self.wbseld, self.t_wbsel),
                             (self.epsd, self.t_eps), (self.twosd, self.t_twos)]:
                inc("ld", sync.dma_start(out=dst[:], in_=src[:]), 16)
            inc("ld", sync.dma_start(
                out=self.t_xp[:].rearrange("p (i c) -> p i c", i=NIC),
                in_=self.xpTp.rearrange("(i p) c -> p i c", i=NIC, p=128)), 16)
            inc("ld", sync.dma_start(
                out=t_gwraw[:].rearrange("p (i c) -> p i c", i=NDC),
                in_=self.gwTd.rearrange("(i p) c -> p i c", i=NDC, p=128)), 16)
        ld0 = V("ld")

        self.t_Apack = t_Apack
        self.ld0v = ld0

        @block.vector
        def _(ve):
            W(ve, "ld", ld0)
            for dc in range(NDC):
                inc("ve", ve.tensor_scalar(
                    self.t_gw[:, dc * NE:(dc + 1) * NE],
                    t_gwraw[:, dc * NE:(dc + 1) * NE],
                    self.t_nwp[:, dc:dc + 1], None, ALU.mult))
        self.ld0 = ld0

    def p1_ln_gate(self):
        block, inc, V, W = self.block, self.inc, self.V, self.W
        nc, sb = self.nc, self.sb
        g1 = ExitStack()
        self.t_xnb = [sb(f"t_xnb{i}", [128, T], BF16, self.gB) for i in range(NDC)]
        t_x = [sb(f"t_x{i}", [128, T], F32, g1) for i in range(NDC)]
        t_xn = t_x
        t_tf = [sb(f"t_tf{i}", [128, T], BF16, g1) for i in range(NDC)]
        t_x2 = [sb(f"t_x2{i}", [128, T], BF16, g1) for i in range(2)]
        _r = [sb(f"t_row{i}", [1, T], F32, g1) for i in range(4)]
        rows = {"m": _r[0], "m2": _r[1], "v": _r[1], "v2": _r[1],
                "sd": _r[2], "rs": _r[2], "sd2": _r[3], "rs2": _r[3]}
        t_logit = sb("t_logit", [NE, T], F32, g1)
        t_rs2tm = sb("t_rs2tm", [128, NTC], F32, g1)
        t_ltm = sb("t_ltm", [128, NTC * NE], F32, g1)
        t_l1pw = sb("t_l1pw", [128, NTC], F32, g1)
        t_l1n = sb("t_l1n", [128, NTC], F32, g1)
        t_l2cw = sb("t_l2cw", [128, NTC * NE], F32, g1)
        t_l2pw = sb("t_l2pw", [128, NTC], F32, g1)
        t_m1w = sb("t_m1w", [128, NTC * NE], F32, g1)
        t_m2w = sb("t_m2w", [128, NTC * NE], F32, g1)
        t_sel2w = sb("t_sel2w", [128, NTC * NE], F32, g1)
        t_qw = sb("t_qw", [128, NTC * NE], F32, g1)
        t_qsw = sb("t_qsw", [128, NTC * NE], F32, g1)
        t_ssumw = sb("t_ssumw", [128, NTC], F32, g1)
        t_rsumw = sb("t_rsumw", [128, NTC], F32, g1)

        @block.sync
        def _(sync):
            for dc in range(NDC):
                inc("lx", sync.dma_start(
                    out=t_x[dc][:], in_=self.x_fm[dc * 128:(dc + 1) * 128, :]), 16)
        ld_all = ("lx", V("lx"))

        with ExitStack() as pp:
            p_sum = self.ps("p_sum", [1, T], pp)
            p_sq = self.ps("p_sq", [1, T], pp)

            se_sq = []
            for dc in range(NDC):
                @block.scalar
                def _(sc, dc=dc):
                    if dc == 0:
                        self.preload_set(sc, [AF.Sqrt, AF.Square])
                    W(sc, *ld_all)
                    if dc >= 2:
                        W(sc, "te", te_sq[dc - 2])
                    inc("se", sc.activation(t_x2[dc % 2][:], t_x[dc][:], AF.Square))
                se_sq.append(V("se"))
                if dc == 0:
                    te_sq = []

                @block.tensor
                def _(te, dc=dc):
                    W(te, *ld_all)
                    if dc == 0:
                        W(te, "ld", self.ld0)
                    for k in range(4):
                        te.matmul(p_sum[:, k * 512:(k + 1) * 512], self.t_onesf[:],
                                  t_x[dc][:, k * 512:(k + 1) * 512],
                                  start=(dc == 0), stop=(dc == NDC - 1))
                    W(te, "se", se_sq[dc])
                    for k in range(4):
                        i_ = te.matmul(p_sq[:, k * 512:(k + 1) * 512], self.t_onesb[:],
                                       t_x2[dc % 2][:, k * 512:(k + 1) * 512],
                                       start=(dc == 0), stop=(dc == NDC - 1))
                    inc("te", i_)
                te_sq.append(V("te"))
            te_stats = V("te")

            @block.vector
            def _(ve):
                W(ve, "te", te_stats)
                inc("ve", ve.tensor_single_scalar(rows["m"][:], p_sum[:], 1.0 / D,
                                                  ALU.mult))
                W(ve, "ve")
                inc("ve", ve.tensor_mul(rows["m2"][:], rows["m"][:], rows["m"][:]))
                W(ve, "ve")
                inc("ve", ve.scalar_tensor_tensor(
                    rows["v"][:], p_sq[:], 1.0 / D, rows["m2"][:],
                    ALU.mult, ALU.subtract))
            ve_v = V("ve")

        @block.scalar
        def _(sc):
            W(sc, "ve", ve_v)
            inc("se", sc.activation(rows["sd"][:], rows["v"][:], AF.Sqrt,
                                    bias=self.t_eps[0:1, 0:1]))
        se_sd = V("se")

        @block.vector
        def _(ve):
            W(ve, "se", se_sd)
            inc("ve", ve.reciprocal(rows["rs"][:], rows["sd"][:]))
            W(ve, "ve")
            inc("ve", ve.tensor_mul(rows["v2"][:], rows["v"][:], rows["rs"][:]))
            W(ve, "ve")
            inc("ve", ve.tensor_mul(rows["v2"][:], rows["v2"][:], rows["rs"][:]))
        ve_v2 = V("ve")

        @block.scalar
        def _(sc):
            W(sc, "ve", ve_v2)
            inc("se", sc.activation(rows["sd2"][:], rows["v2"][:], AF.Sqrt,
                                    bias=self.t_eps[0:1, 1:2]))
        se_sd2 = V("se")

        @block.vector
        def _(ve):
            W(ve, "se", se_sd2)
            inc("ve", ve.reciprocal(rows["rs2"][:], rows["sd2"][:]))
        ve_rs2 = V("ve")

        with ExitStack() as pp:
            p_mbc = self.ps("p_mbc", [128, T], pp)
            p_rbc = self.ps("p_rbc", [128, T], pp)

            @block.tensor
            def _(te):
                W(te, "ve", ve_rs2)
                for k in range(4):
                    te.matmul(p_mbc[:, k * 512:(k + 1) * 512], self.t_onescf[:],
                              rows["m"][:, k * 512:(k + 1) * 512],
                              start=True, stop=True)
                for k in range(4):
                    i_ = te.matmul(p_rbc[:, k * 512:(k + 1) * 512], self.t_onescf[:],
                                   rows["rs"][:, k * 512:(k + 1) * 512],
                                   start=True, stop=True)
                inc("te", i_)
            te_bc = V("te")

            for dc in range(NDC):
                @block.vector
                def _(ve, dc=dc):
                    if dc == 0:
                        W(ve, "te", te_bc)
                    inc("ve", ve.tensor_sub(t_xn[dc][:], t_x[dc][:], p_mbc[:]))
                    W(ve, "ve")
                    inc("ve", ve.tensor_mul(t_xn[dc][:], t_xn[dc][:], p_rbc[:]))
                    W(ve, "ve")
                    inc("ve", ve.tensor_copy(self.t_xnb[dc][:], t_xn[dc][:]))
            ve_xn = V("ve")

        with ExitStack() as pp:
            p_log = self.ps("p_log", [NE, T], pp)

            @block.tensor
            def _(te):
                W(te, "ve", ve_xn)
                for dc in range(NDC):
                    for k in range(4):
                        i_ = te.matmul(p_log[:, k * 512:(k + 1) * 512],
                                       self.t_gw[:, dc * NE:(dc + 1) * NE],
                                       t_xn[dc][:, k * 512:(k + 1) * 512],
                                       start=(dc == 0), stop=(dc == NDC - 1))
                inc("te", i_)
            te_log = V("te")

            @block.scalar
            def _(sc):
                W(sc, "te", te_log)
                inc("se", sc.copy(t_logit[:], p_log[:]))
            se_log = V("se")

        with ExitStack() as pp:
            p_lt = self.ps("p_lt", [128, NE], pp)
            p_rt = self.ps("p_rt", [128, 1], pp)
            for tc in range(NTC):
                @block.tensor
                def _(te, tc=tc):
                    if tc == 0:
                        W(te, "se", se_log)
                    else:
                        W(te, "ve", V("ve"))
                    te.transpose(p_lt[:], t_logit[:, tc * 128:(tc + 1) * 128],
                                 self.t_eyef[0:NE, 0:NE])
                    i_ = te.transpose(p_rt[:],
                                      rows["rs2"][:, tc * 128:(tc + 1) * 128],
                                      self.t_eyef[0:1, 0:1])
                    inc("te", i_)
                te_t = V("te")

                @block.vector
                def _(ve, tc=tc, te_t=te_t):
                    W(ve, "te", te_t)
                    inc("ve", ve.tensor_copy(t_rs2tm[:, tc:tc + 1], p_rt[:]))
                    W(ve, "ve")
                    inc("ve", ve.tensor_scalar(
                        t_ltm[:, tc * NE:(tc + 1) * NE], p_lt[:],
                        t_rs2tm[:, tc:tc + 1], None, ALU.mult))

        for tc in range(NTC):
            s8 = slice(tc * NE, (tc + 1) * NE)
            c1 = slice(tc, tc + 1)

            @block.vector
            def _(ve, tc=tc, s8=s8, c1=c1):
                lt = t_ltm[:, s8]
                W(ve, "ve")
                inc("ve", ve.reduce_max(t_l1pw[:, c1], lt, axis=AX.X))
                inc("ve", ve.reduce_max(t_l1n[:, c1], lt, axis=AX.X,
                                        negate=True))
                W(ve, "ve")
                inc("ve", ve.tensor_single_scalar(t_m1w[:, s8], lt,
                                                  t_l1pw[:, c1], ALU.is_ge))
                W(ve, "ve")
                inc("ve", ve.scalar_tensor_tensor(t_l2cw[:, s8], t_m1w[:, s8],
                                                  -1e30, lt,
                                                  ALU.mult, ALU.add))
                W(ve, "ve")
                inc("ve", ve.reduce_max(t_l2pw[:, c1], t_l2cw[:, s8], axis=AX.X))
                W(ve, "ve")
                inc("ve", ve.tensor_single_scalar(t_m2w[:, s8], t_l2cw[:, s8],
                                                  t_l2pw[:, c1], ALU.is_ge))
                W(ve, "ve")
                inc("ve", ve.tensor_add(t_sel2w[:, s8], t_m1w[:, s8],
                                        t_m2w[:, s8]))
            ve_sel = V("ve")

            @block.scalar
            def _(sc, tc=tc, s8=s8, c1=c1, ve_sel=ve_sel):
                if tc == 0:
                    self.preload_set(sc, [AF.Exp])
                W(sc, "ve", ve_sel)
                inc("se", sc.activation(t_qw[:, s8], t_ltm[:, s8],
                                        AF.Exp, bias=t_l1n[:, c1]))
            se_q = V("se")

            @block.vector
            def _(ve, tc=tc, s8=s8, c1=c1, se_q=se_q):
                W(ve, "se", se_q)
                inc("ve", ve.tensor_mul(t_qsw[:, s8], t_qw[:, s8],
                                        t_sel2w[:, s8]))
                W(ve, "ve")
                inc("ve", ve.reduce_sum(t_ssumw[:, c1], t_qsw[:, s8], axis=AX.X))
                W(ve, "ve")
                inc("ve", ve.reciprocal(t_rsumw[:, c1], t_ssumw[:, c1]))
                W(ve, "ve")
                inc("ve", ve.tensor_scalar(
                    self.t_wtm[:, s8], t_qsw[:, s8],
                    t_rsumw[:, c1], None, ALU.mult))
        ve_w = V("ve")

        with ExitStack() as pp:
            p_r2 = self.ps("p_r2", [128, T], pp)

            @block.tensor
            def _(te):
                W(te, "ve", ve_w)
                for k in range(4):
                    i_ = te.matmul(p_r2[:, k * 512:(k + 1) * 512], self.t_onescf[:],
                                   rows["rs2"][:, k * 512:(k + 1) * 512],
                                   start=True, stop=True)
                inc("te", i_)
            te_r2 = V("te")

            for dc in range(NDC):
                @block.vector
                def _(ve, dc=dc):
                    if dc == 0:
                        W(ve, "te", te_r2)
                    inc("ve", ve.scalar_tensor_tensor(
                        t_tf[dc][:], t_xn[dc][:], self.t_nwp[:, dc:dc + 1],
                        p_r2[:], ALU.mult, ALU.mult))
            ve_tf = V("ve")

            @block.sync
            def _(sync):
                W(sync, "ve", ve_tf)
                for dc in range(NDC):
                    inc("st", sync.dma_start(out=self.sp_tfm[dc], in_=t_tf[dc][:]),
                        16)
            st_tf = V("st")

        if "xn" in self.taps:
            o = self.dout("dbg_xn", (NDC, 128, T))

            @block.sync
            def _(sync):
                W(sync, "ve", ve_tf)
                for dc in range(NDC):
                    inc("sdbg", sync.dma_start(out=o[dc], in_=t_xn[dc][:]), 16)
        if "wtm" in self.taps:
            o = self.dout("dbg_wtm", (128, NTC * NE))
            o2 = self.dout("dbg_ltm", (128, NTC * NE))
            o3 = self.dout("dbg_logit", (NE, T))
            o4 = self.dout("dbg_rs2tm", (128, NTC))
            extra = {}
            for nm, tt in [("l1p", t_l1pw), ("l1n", t_l1n), ("m1", t_m1w),
                           ("l2c", t_l2cw), ("l2p", t_l2pw), ("m2", t_m2w),
                           ("sel2", t_sel2w), ("q", t_qw), ("qs", t_qsw),
                           ("ssum", t_ssumw), ("rsum", t_rsumw)]:
                extra[nm] = (self.dout("dbg_" + nm, tuple(tt.shape)), tt)

            @block.sync
            def _(sync):
                W(sync, "ve", ve_w)
                inc("sdbg", sync.dma_start(out=o[:], in_=self.t_wtm[:]), 16)
                inc("sdbg", sync.dma_start(out=o2[:], in_=t_ltm[:]), 16)
                inc("sdbg", sync.dma_start(out=o3[:], in_=t_logit[:]), 16)
                inc("sdbg", sync.dma_start(out=o4[:], in_=t_rs2tm[:]), 16)
                for nm, (oo, tt) in extra.items():
                    inc("sdbg", sync.dma_start(out=oo[:], in_=tt[:]), 16)
        if "tfm" in self.taps:
            o = self.dout("dbg_tfm", (NDC, 128, T), BF16)

            @block.sync
            def _(sync):
                W(sync, "st", st_tf)
                for dc in range(NDC):
                    inc("sdbg", sync.dma_start(out=o[dc], in_=self.sp_tfm[dc]), 16)

        @block.vector
        def _(ve):
            W(ve, "st", st_tf)
            if "sdbg" in self.sems:
                W(ve, "sdbg", V("sdbg"))
        g1.close()

    def p2_inproj_conv(self):
        block, inc, V, W = self.block, self.inc, self.V, self.W
        sb = self.sb
        t_Bfm = sb("t_Bfm", [NST, T], BF16, self.gB)
        t_Cfm = sb("t_Cfm", [NST, T], BF16, self.gB)
        g2 = ExitStack()
        t_slab = [sb(f"t_slab{i}", [128, 1024], BF16, g2) for i in range(2)]
        t_xin = [sb(f"t_xin{i}", [128, T + 3], BF16, g2) for i in range(2)]
        t_acc = [sb(f"t_acc{i}", [128, T], F32, g2) for i in range(2)]
        t_xcv = [sb(f"t_xcv{i}", [128, T], BF16, g2) for i in range(2)]
        t_zz = [sb(f"t_zz{i}", [128, T], BF16, g2) for i in range(2)]

        gproj = ExitStack()
        p_proj = self.ps("p_proj", [112, T], gproj)
        p_xz = [self.ps(f"p_xz{i}", [128, 1024], g2) for i in range(2)]

        @block.vector
        def _(ve):
            inc("ve", ve.memset(t_xin[0][:, 0:3], 0.0))
            inc("ve", ve.memset(t_xin[1][:, 0:3], 0.0))
        ve_pad = V("ve")

        te_consumed_slab = {}
        se_half = {}
        cons_done = {}
        for ic in range(2 * NIC):
            bb = ic % 2

            @block.sync
            def _(sync, ic=ic, bb=bb):
                if ic < 2 and "sdbg" in self.sems:
                    W(sync, "sdbg", V("sdbg"))
                if ic >= 2:
                    W(sync, "te", te_consumed_slab[ic - 2])
                inc(f"lw{bb}", sync.dma_start(
                    out=t_slab[bb][:].rearrange("p (i c) -> p i c", i=NDC),
                    in_=self.ipT.rearrange("(i p) c -> p i c", i=NDC, p=128)[
                        :, :, ic * 128:(ic + 1) * 128]), 16)
            ld_slab = V(f"lw{bb}")

            for half in range(2):
                @block.tensor
                def _(te, ic=ic, bb=bb, half=half, ld_slab=ld_slab):
                    W(te, f"lw{bb}", ld_slab)
                    if ic >= 1:
                        W(te, "se", se_half[(ic - 1, half)])
                    if ic < NIC and half == 0:
                        if ic >= 2:
                            W(te, "ve", cons_done[ic - 2])
                    if ic >= NIC and half == 0 and ic - 2 >= NIC:
                        W(te, *cons_done[ic - 2])
                    for dc in range(NDC):
                        for k in range(2):
                            i_ = te.matmul(
                                p_xz[half][:, k * 512:(k + 1) * 512],
                                t_slab[bb][:, dc * 128:(dc + 1) * 128],
                                self.t_xnb[dc][:, half * 1024 + k * 512:
                                               half * 1024 + (k + 1) * 512],
                                start=(dc == 0), stop=(dc == NDC - 1))
                    inc("te", i_)
                te_mm = V("te")

                @block.scalar
                def _(sc, ic=ic, bb=bb, half=half, te_mm=te_mm):
                    if ic == 0 and half == 0:
                        W(sc, "ve", ve_pad)
                    W(sc, "te", te_mm)
                    if ic < NIC:
                        dst = t_xin[bb][:, 3 + half * 1024:3 + (half + 1) * 1024]
                    else:
                        dst = t_zz[bb][:, half * 1024:(half + 1) * 1024]
                    inc("se", sc.copy(dst, p_xz[half][:]))
                se_half[(ic, half)] = V("se")
            te_consumed_slab[ic] = V("te")

            if ic < NIC:
                @block.vector
                def _(ve, ic=ic, bb=bb):
                    W(ve, "se", se_half[(ic, 1)])
                    if ic >= 2:
                        W(ve, "se", se_silu[ic - 2])
                    xin, acc = t_xin[bb], t_acc[bb]
                    inc("ve", ve.tensor_scalar(
                        acc[:], xin[:, 3:3 + T],
                        self.t_convw[:, ic * DCONV + 3:ic * DCONV + 4], None,
                        ALU.mult))
                    for k in (2, 1, 0):
                        W(ve, "ve")
                        inc("ve", ve.scalar_tensor_tensor(
                            acc[:], xin[:, k:k + T],
                            self.t_convw[:, ic * DCONV + k:ic * DCONV + k + 1],
                            acc[:], ALU.mult, ALU.add))
                ve_conv = V("ve")
                cons_done[ic] = ve_conv
                if ic == 0:
                    se_silu = {}

                @block.scalar
                def _(sc, ic=ic, bb=bb, ve_conv=ve_conv):
                    if ic == 0:
                        self.preload_set(sc, [AF.Silu])
                    W(sc, "ve", ve_conv)
                    if ic >= 2:
                        W(sc, *gd_xcv[ic - 2])
                        W(sc, "te", te_xp[ic - 2])
                    inc("se", sc.activation(t_xcv[bb][:], t_acc[bb][:], AF.Silu,
                                            bias=self.t_cbcol[:, ic:ic + 1]))
                se_silu[ic] = V("se")
                if ic == 0:
                    gd_xcv, te_xp = {}, {}

                @block.gpsimd
                def _(gp, ic=ic, bb=bb):
                    W(gp, "se", se_silu[ic])
                    inc(f"sxc{bb}", gp.dma_start(out=self.sp_xconv[ic],
                                                 in_=t_xcv[bb][:]), 16)
                gd_xcv[ic] = (f"sxc{bb}", V(f"sxc{bb}"))

                @block.tensor
                def _(te, ic=ic, bb=bb):
                    W(te, "se", se_silu[ic])
                    for k in range(4):
                        i_ = te.matmul(
                            p_proj[:, k * 512:(k + 1) * 512],
                            self.t_xp[:, ic * 112:(ic + 1) * 112],
                            t_xcv[bb][:, k * 512:(k + 1) * 512],
                            start=(ic == 0), stop=(ic == NIC - 1))
                    inc("te", i_)
                te_xp[ic] = V("te")
            else:
                @block.gpsimd
                def _(gp, ic=ic, bb=bb):
                    W(gp, "se", se_half[(ic, 1)])
                    inc(f"sz{bb}", gp.dma_start(out=self.sp_z[ic - NIC],
                                                in_=t_zz[bb][:]), 16)
                cons_done[ic] = (f"sz{bb}", V(f"sz{bb}"))
        te_proj = te_xp[NIC - 1]

        @block.scalar
        def _(sc):
            W(sc, "te", te_proj)
            inc("se", sc.copy(self.t_dtraw[:], p_proj[0:64, :]))
            inc("se", sc.copy(t_Bfm[:], p_proj[64:80, :]))
            inc("se", sc.copy(t_Cfm[:], p_proj[96:112, :]))
        se_proj = V("se")
        self.se_proj = se_proj

        @block.sync
        def _(sync):
            W(sync, "se", se_proj)
            inc("st", sync.dma_start(out=self.sp_bc[0], in_=t_Bfm[:]), 16)
            inc("st", sync.dma_start(out=self.sp_bc[1], in_=t_Cfm[:]), 16)
        self.st_bc = V("st")
        self.spill_p2 = [(nm, V(nm)) for nm in
                         ["sxc0", "sxc1", "sz0", "sz1"]]
        g2.close()
        gproj.close()
        self.gB.close()

        if "xconv" in self.taps:
            o = self.dout("dbg_xconv", (NIC, 128, T), BF16)

            @block.sync
            def _(sync):
                for nm, v in self.spill_p2:
                    W(sync, nm, v)
                for ic in range(NIC):
                    inc("sdbg", sync.dma_start(out=o[ic], in_=self.sp_xconv[ic]), 16)
        if "z" in self.taps:
            o = self.dout("dbg_z", (NIC, 128, T), BF16)

            @block.sync
            def _(sync):
                for nm, v in self.spill_p2:
                    W(sync, nm, v)
                for ic in range(NIC):
                    inc("sdbg", sync.dma_start(out=o[ic], in_=self.sp_z[ic]), 16)
        if "proj" in self.taps:
            o1 = self.dout("dbg_dtraw", (DTR, T), BF16)

            @block.sync
            def _(sync):
                W(sync, "se", se_proj)
                inc("sdbg", sync.dma_start(out=o1[:], in_=self.t_dtraw[:]), 16)

    def p4_init(self):
        from types import SimpleNamespace
        block, inc, V, W = self.block, self.inc, self.V, self.W
        sb = self.sb
        g4 = self.g4
        P = SimpleNamespace()
        self.P4 = P
        P.t_et = sb("t_et", [128, QW], BF16, g4)
        P.t_delta = [sb(f"t_delta{i}", [128, QW], BF16, g4) for i in range(2)]
        P.t_du = [sb(f"t_du{i}", [128, QW], BF16, g4) for i in range(2)]
        P.t_xcr = [sb(f"t_xcr{i}", [128, QW], BF16, g4) for i in range(2)]
        P.t_a = [sb(f"t_a{i}", [128, QW + 2], BF16, g4)
                 for i in range(NST)]
        P.t_b = [sb("t_b0", [128, QW + 2], BF16, g4)] * 2
        P.t_h = [sb(f"t_h{i}", [128, QW + 2], BF16, g4) for i in range(2)]
        P.t_hC = [sb(f"t_hC{i}", [128, QW], BF16, g4) for i in range(2)]
        P.t_y1 = [sb(f"t_y1{i}", [128, QW], BF16, g4) for i in range(2)]
        P.t_carry = sb("t_carry", [128, NIC * NST], BF16, g4)
        P.t_Bq = [sb(f"t_Bq{n}", [128, QW], BF16, g4) for n in range(NST)]
        P.t_Cq = [sb(f"t_Cq{n}", [128, QW], BF16, g4) for n in range(NST)]

        P.dbg_delta = None
        if "delta" in self.taps:
            P.dbg_delta = self.dout("dbg_delta", (NIC, 128, T), BF16)

        def bc_pair(sync, q, n):
            inc("bcq", sync.dma_start(
                out=P.t_Bq[n][:],
                in_=self.sp_bc[0][n:n + 1, q * QW:(q + 1) * QW]
                .broadcast_to((128, QW))), 16)
            inc("bcq", sync.dma_start(
                out=P.t_Cq[n][:],
                in_=self.sp_bc[1][n:n + 1, q * QW:(q + 1) * QW]
                .broadcast_to((128, QW))), 16)
        P.bc_pair = bc_pair

        @block.sync
        def _(sync):
            W(sync, "st", self.st_bc)
            W(sync, "se", self.se_proj)
            for nm, v in self.spill_p2:
                W(sync, nm, v)
            for n in range(NST):
                bc_pair(sync, 0, n)
        P.bcq_full = {0: V("bcq")}

        @block.vector
        def _(ve):
            W(ve, "st", self.st_bc)
            for nm, v in self.spill_p2:
                W(ve, nm, v)
            for i in range(NST):
                inc("ve", ve.memset(P.t_a[i][:, 1:2], 1.0))
        P.ve_ainit = V("ve")

        P.pp = ExitStack()
        P.p_dl = self.ps("p_dl", [128, QW], P.pp)
        P.p_y = self.ps("p_y", [128, QW], P.pp)
        P.se_et, P.se_dl, P.ve_du, P.ve_y1, P.st_y1 = {}, {}, {}, {}, {}
        P.se_a, P.ve_b, P.ve_scan, P.gp_hC, P.te_y = {}, {}, {}, {}, {}
        P.ldx, P.te_dl = {}, {}
        P.kv, P.nv, P.pend, P.credits = 0, 0, None, 0.0

    def p4_piece(self, k):
        self.p4_xcr(k)
        self.p4_dt(k)
        self.p4_exps(k)
        for n in range(NST):
            self.p4_ngroup(k, n)
            if n >= 1:
                self.p4_ngroup_te(k, n - 1)
        self.p4_ngroup_te(k, NST - 1)
        self.p4_y1(k)

    def p4_xcr(self, k):
        block, inc, V, W = self.block, self.inc, self.V, self.W
        P = self.P4
        q, ic = divmod(k, NIC)
        icb = ic % 2

        @block.sync
        def _(sync, q=q, ic=ic, icb=icb):
            pv = ((q, ic - 2) if ic >= 2 else
                  (q - 1, NIC - 2 + ic) if q >= 1 else None)
            if pv is not None:
                W(sync, "ve", P.ve_y1[pv])
            else:
                for nm, v in self.spill_p2:
                    W(sync, nm, v)
            inc(f"lxc{icb}", sync.dma_start(
                out=P.t_xcr[icb][:],
                in_=self.sp_xconv[ic][:, q * QW:(q + 1) * QW]), 16)
        P.ldx[(q, ic)] = (f"lxc{icb}", V(f"lxc{icb}"))

    def p4_dt(self, k):
        block, inc, V, W = self.block, self.inc, self.V, self.W
        P = self.P4
        q, ic = divmod(k, NIC)

        @block.tensor
        def _(te, q=q, ic=ic):
            pv = ((q, ic - 1) if ic >= 1 else
                  (q - 1, NIC - 1) if q >= 1 else None)
            if pv is None:
                W(te, "se", self.se_proj)
            else:
                W(te, "se", P.se_et[pv])
            i_ = te.matmul(P.p_dl[:],
                           self.t_dtT[:, ic * 128:(ic + 1) * 128],
                           self.t_dtraw[:, q * QW:(q + 1) * QW],
                           start=True, stop=True)
            inc("te", i_)
        P.te_dl[(q, ic)] = V("te")

    def p4_exps(self, k):
        self.p4_exps_a(k)
        self.p4_exps_b(k)

    def p4_exps_a(self, k):
        block, inc, V, W = self.block, self.inc, self.V, self.W
        P = self.P4
        q, ic = divmod(k, NIC)
        icb = ic % 2
        se_dl, ve_du, ve_scan = P.se_dl, P.ve_du, P.ve_scan

        @block.scalar
        def _(sc, q=q, ic=ic, icb=icb):
            self.preload_set(sc, [AF.Exp, AF.Ln])
            if q == 0 and ic == 0:
                inc("se", sc.activation(self.t_Af[:],
                                        self.t_Apack[:], AF.Exp))
            W(sc, "te", P.te_dl[(q, ic)])
            inc("se", sc.activation(
                P.t_et[:], P.p_dl[:], AF.Exp,
                bias=self.t_dtbcol[:, ic:ic + 1]))
        P.se_et[(q, ic)] = V("se")

        if q == 0 and ic == 0:
            @block.vector
            def _(ve):
                W(ve, "se", P.se_et[(0, 0)])
                W(ve, "ve", P.ve_ainit)
                inc("ve", ve.tensor_single_scalar(
                    self.t_Af[:], self.t_Af[:], -1.0, ALU.mult))
            P.ve_neg = V("ve")

        @block.scalar
        def _(sc, q=q, ic=ic, icb=icb):
            pv = ((q, ic - 2) if ic >= 2 else
                  (q - 1, NIC - 2 + ic) if q >= 1 else None)
            if pv is not None:
                W(sc, "ve", ve_du[pv])
            W(sc, "se")
            inc("se", sc.activation(P.t_delta[icb][:], P.t_et[:],
                                    AF.Ln, bias=1.0))
        se_dl[(q, ic)] = V("se")

        if P.dbg_delta is not None:
            @block.sync
            def _(sync, q=q, ic=ic, icb=icb):
                W(sync, "se", se_dl[(q, ic)])
                inc("sdbg", sync.dma_start(
                    out=P.dbg_delta[ic][:, q * QW:(q + 1) * QW],
                    in_=P.t_delta[icb][:]), 16)
            sdbg_now = V("sdbg")

            @block.scalar
            def _(sc, sdbg_now=sdbg_now):
                W(sc, "sdbg", sdbg_now)

        self._p4_a_exps(k, 0, 7)

    def p4_exps_b(self, k):
        self._p4_a_exps(k, 7, NST)

    def _p4_a_exps(self, k, n0, n1):
        block, inc, V, W = self.block, self.inc, self.V, self.W
        P = self.P4
        q, ic = divmod(k, NIC)
        icb = ic % 2
        se_dl, ve_scan = P.se_dl, P.ve_scan
        for n in range(n0, n1):
            pvp = ((q, ic - 1, n) if ic >= 1 else
                   (q - 1, NIC - 1, n) if q >= 1 else None)

            @block.scalar
            def _(sc, q=q, ic=ic, n=n, icb=icb, pvp=pvp):
                self.preload_set(sc, [AF.Exp, AF.Ln])
                if q == 0 and ic == 0 and n == 0:
                    W(sc, "ve", P.ve_neg)
                if pvp is not None:
                    W(sc, "ve", ve_scan[pvp])
                W(sc, "se", se_dl[(q, ic)])
                inc("se", sc.activation(
                    P.t_a[n][:, 2:QW + 2], P.t_delta[icb][:], AF.Exp,
                    scale=self.t_Af[:,
                                    ic * NST + n:ic * NST + n + 1]))
            P.se_a[(q, ic, n)] = V("se")

    def p4_du(self, k):
        block, inc, V, W = self.block, self.inc, self.V, self.W
        P = self.P4
        q, ic = divmod(k, NIC)
        icb = ic % 2

        @block.vector
        def _(ve, q=q, ic=ic, icb=icb):
            W(ve, "se", P.se_dl[(q, ic)])
            W(ve, *P.ldx[(q, ic)])
            inc("ve", ve.tensor_mul(P.t_du[icb][:], P.t_delta[icb][:],
                                    P.t_xcr[icb][:]))
        P.ve_du[(q, ic)] = V("ve")

    def p4_ngroup(self, k, n):
        block, inc, V, W = self.block, self.inc, self.V, self.W
        P = self.P4
        q, ic = divmod(k, NIC)
        icb = ic % 2
        if n == 0:
            self.p4_du(k)
        nb2 = n % 2
        t_a, t_b, t_h, t_hC = P.t_a, P.t_b, P.t_h, P.t_hC
        t_carry, t_Bq, t_Cq = P.t_carry, P.t_Bq, P.t_Cq
        ve_scan, gp_hC, te_y = P.ve_scan, P.gp_hC, P.te_y
        pv2 = ((q, ic, n - 2) if n >= 2 else
               (q, ic - 1, NST - 2 + n) if ic >= 1 else
               (q - 1, NIC - 1, NST - 2 + n) if q >= 1
               else None)

        @block.vector
        def _(ve, q=q, ic=ic, n=n, icb=icb, nb2=nb2):
            if ic == 0 and n == 0:
                W(ve, "bcq", P.bcq_full[q])
            if n == 0:
                W(ve, "ve")
            inc("ve", ve.tensor_mul(t_b[nb2][:, 2:QW + 2],
                                    P.t_du[icb][:], t_Bq[n][:]))
            if q == 0:
                inc("ve", ve.memset(t_b[nb2][:, 1:2], 0.0))
            else:
                inc("ve", ve.tensor_copy(
                    t_b[nb2][:, 1:2],
                    t_carry[:, ic * NST + n:ic * NST + n + 1]))
        P.ve_b[(q, ic, n)] = V("ve")

        @block.vector
        def _(ve, q=q, ic=ic, n=n, nb2=nb2):
            W(ve, "se", P.se_a[(q, ic, n)])
            W(ve, "ve")
            inc("ve", ve.tensor_tensor_scan(
                t_h[nb2][:, 1:QW + 2], t_a[n][:, 1:QW + 2],
                t_b[nb2][:, 1:QW + 2], 0.0,
                ALU.mult, ALU.add))
            if q < NQ - 1:
                W(ve, "ve")
                inc("ve", ve.tensor_copy(
                    t_carry[:, ic * NST + n:ic * NST + n + 1],
                    t_h[nb2][:, QW + 1:QW + 2]))
        ve_scan[(q, ic, n)] = V("ve")

        @block.vector
        def _(ve, q=q, ic=ic, n=n, nb2=nb2, pv2=pv2):
            if pv2 is not None:
                W(ve, "te", te_y[pv2])
            if q == NQ - 1:
                W(ve, "ve")
            inc("ve", ve.tensor_mul(t_hC[nb2][:],
                                    t_h[nb2][:, 2:QW + 2],
                                    t_Cq[n][:]))
        gp_hC[(q, ic, n)] = V("ve")

        if ic == NIC - 1 and q < NQ - 1:
            @block.sync
            def _(sync, q=q, n=n):
                W(sync, "ve", gp_hC[(q, NIC - 1, n)])
                P.bc_pair(sync, q + 1, n)
            if n == NST - 1:
                P.bcq_full[q + 1] = V("bcq")

    def p4_ngroup_te(self, k, n):
        block, inc, V, W = self.block, self.inc, self.V, self.W
        P = self.P4
        q, ic = divmod(k, NIC)
        nb2 = n % 2
        te_y = P.te_y

        @block.tensor
        def _(te, q=q, ic=ic, n=n, nb2=nb2):
            W(te, "ve", P.gp_hC[(q, ic, n)])
            if n == 0:
                pv = ((q, ic - 1) if ic >= 1 else
                      (q - 1, NIC - 1) if q >= 1 else None)
                if pv is not None:
                    W(te, "ve", P.ve_y1[pv])
            i_ = te.matmul(P.p_y[:], self.t_eyeb[:],
                           P.t_hC[nb2][:],
                           start=(n == 0), stop=(n == NST - 1))
            inc("te", i_)
        te_y[(q, ic, n)] = V("te")

    def p4_y1(self, k):
        block, inc, V, W = self.block, self.inc, self.V, self.W
        P = self.P4
        q, ic = divmod(k, NIC)
        icb = ic % 2

        @block.vector
        def _(ve, q=q, ic=ic, icb=icb):
            yb = P.t_y1[icb]
            pv = ((q, ic - 2) if ic >= 2 else
                  (q - 1, NIC - 2 + ic) if q >= 1 else None)
            if pv is not None:
                W(ve, *P.st_y1[pv])
            inc("ve", ve.tensor_scalar(
                yb[:], P.t_xcr[icb][:], self.t_dcol[:, ic:ic + 1],
                None, ALU.mult))
            W(ve, "te", P.te_y[(q, ic, NST - 1)])
            W(ve, "ve")
            inc("ve", ve.scalar_tensor_tensor(
                yb[:], P.p_y[:], 1.0, yb[:], ALU.mult, ALU.add))
        P.ve_y1[(q, ic)] = V("ve")

        @block.gpsimd
        def _(gp, q=q, ic=ic, icb=icb):
            W(gp, "ve", P.ve_y1[(q, ic)])
            inc(f"sy{icb}", gp.dma_start(
                out=self.sp_y1[ic][:, q * QW:(q + 1) * QW],
                in_=P.t_y1[icb][:]), 16)
        P.st_y1[(q, ic)] = (f"sy{icb}", V(f"sy{icb}"))

    def _p4_step(self):
        P = self.P4
        if P.pend is not None:
            pk, pn = P.pend
            self.p4_ngroup_te(pk, pn)
            if pn == NST - 1:
                self.p4_y1(pk)
                if pk + 1 < NQ * NIC:
                    self.p4_exps_b(pk + 1)
            P.pend = None
        if P.kv >= NQ * NIC:
            return False
        self.p4_ngroup(P.kv, P.nv)
        P.pend = (P.kv, P.nv)
        if P.nv == 7 and P.kv + 1 < NQ * NIC:
            self.p4_xcr(P.kv + 1)
            self.p4_dt(P.kv + 1)
            self.p4_exps_a(P.kv + 1)
        P.nv += 1
        if P.nv == NST:
            P.nv = 0
            P.kv += 1
        return True

    def p4_advance(self, credit):
        P = self.P4
        P.credits += credit
        while P.credits >= 1.0:
            P.credits -= 1.0
            if not self._p4_step():
                break

    def p4_close(self):
        block, inc, V, W = self.block, self.inc, self.V, self.W
        while self._p4_step():
            pass
        self.P4.pp.close()
        self.g4.close()
        if "y1" in self.taps:
            o = self.dout("dbg_y1", (NIC, 128, T), BF16)

            @block.sync
            def _(sync):
                W(sync, "sy0", V("sy0"))
                W(sync, "sy1", V("sy1"))
                for ic in range(NIC):
                    inc("sdbg", sync.dma_start(out=o[ic], in_=self.sp_y1[ic]), 16)

    def p5_outproj(self):
        block, inc, V, W = self.block, self.inc, self.V, self.W
        sb = self.sb
        g5 = ExitStack()
        t_oslab = [sb(f"t_oslab{i}", [128, 1024], BF16, g5) for i in range(2)]
        t_y1r = [sb(f"t_y1r{i}", [128, 512], BF16, g5) for i in range(2)]
        t_zr = [sb(f"t_zr{i}", [128, 512], BF16, g5) for i in range(2)]
        t_sz = [sb(f"t_sz{i}", [128, 512], BF16, g5) for i in range(2)]
        t_y2 = [sb(f"t_y2{i}", [128, 512], BF16, g5) for i in range(2)]
        t_sev = [sb(f"t_sev{i}", [128, 512], BF16, g5) for i in range(NDC)]
        p_s = [self.ps(f"p_s{i}", [128, 512], g5) for i in range(NDC)]

        sy_all = [("sy0", V("sy0")), ("sy1", V("sy1")),
                  ("sz0", V("sz0")), ("sz1", V("sz1"))]
        se_sev = 0
        ve_y2t, te_op = {}, {}
        for tb in range(4):
            for ic in range(NIC):
                j = ic % 2
                step = tb * NIC + ic

                @block.sync
                def _(sync, tb=tb, ic=ic, j=j, step=step):
                    if step == 0:
                        for nm, v in sy_all:
                            W(sync, nm, v)
                    if step >= 2:
                        W(sync, "ve", ve_y2t[step - 2])
                    inc(f"lyz{j}", sync.dma_start(
                        out=t_y1r[j][:],
                        in_=self.sp_y1[ic][:, tb * 512:(tb + 1) * 512]), 16)
                    inc(f"lyz{j}", sync.dma_start(
                        out=t_zr[j][:],
                        in_=self.sp_z[ic][:, tb * 512:(tb + 1) * 512]), 16)
                ldx_now = (f"lyz{j}", V(f"lyz{j}"))

                @block.sync
                def _(sync, ic=ic, j=j, step=step):
                    if step < 2:
                        for nm, v in sy_all:
                            W(sync, nm, v)
                    else:
                        W(sync, "te", te_op[step - 2])
                    inc(f"lo{j}", sync.dma_start(
                        out=t_oslab[j][:],
                        in_=self.opT[ic * 128:(ic + 1) * 128, :]), 16)
                ldw_now = (f"lo{j}", V(f"lo{j}"))

                @block.scalar
                def _(sc, j=j, ldx_now=ldx_now, step=step):
                    if step == 0:
                        self.preload_set(sc, [AF.Silu])
                    W(sc, *ldx_now)
                    if step >= 2:
                        W(sc, "ve", ve_y2t[step - 2])
                    inc("se", sc.activation(t_sz[j][:], t_zr[j][:], AF.Silu))
                se_sz = V("se")

                @block.vector
                def _(ve, j=j, se_sz=se_sz, step=step):
                    W(ve, "se", se_sz)
                    if step >= 2:
                        W(ve, "te", te_op[step - 2])
                    inc("ve", ve.tensor_mul(t_y2[j][:], t_y1r[j][:], t_sz[j][:]))
                ve_y2t[step] = V("ve")

                @block.tensor
                def _(te, tb=tb, ic=ic, j=j, ldw_now=ldw_now, step=step,
                      se_sev=se_sev):
                    W(te, *ldw_now)
                    W(te, "ve", ve_y2t[step])
                    if ic == 0 and tb > 0:
                        W(te, "se", se_sev)
                    for dc in range(NDC):
                        i_ = te.matmul(p_s[dc][:],
                                       t_oslab[j][:, dc * 128:(dc + 1) * 128],
                                       t_y2[j][:],
                                       start=(ic == 0), stop=(ic == NIC - 1))
                    inc("te", i_)
                te_op[step] = V("te")

            @block.scalar
            def _(sc, tb=tb):
                W(sc, "te", te_op[tb * NIC + NIC - 1])
                if tb > 0:
                    W(sc, "ssv", V("ssv"))
                for dc in range(NDC):
                    inc("se", sc.copy(t_sev[dc][:], p_s[dc][:]))
            se_sev = V("se")

            @block.sync
            def _(sync, tb=tb, se_sev=se_sev):
                W(sync, "se", se_sev)
                for dc in range(NDC):
                    inc("ssv", sync.dma_start(
                        out=self.sp_s[dc][:, tb * 512:(tb + 1) * 512],
                        in_=t_sev[dc][:]), 16)
        self.st_s = ("ssv", V("ssv"))

        @block.vector
        def _(ve):
            W(ve, *self.st_s)
        g5.close()

        if "s" in self.taps:
            o = self.dout("dbg_s", (NDC, 128, T), BF16)

            @block.sync
            def _(sync):
                W(sync, *self.st_s)
                for dc in range(NDC):
                    inc("sdbg", sync.dma_start(out=o[dc], in_=self.sp_s[dc]), 16)

    def p6_ffn_prep(self):
        block, inc, V, W = self.block, self.inc, self.V, self.W
        sb, g67 = self.sb, self.g67
        self.t_tfmg = [sb(f"t_tfmg{i}", [128, 512], BF16, g67)
                       for i in range(NDC)]
        self.t_b1s = [sb(f"t_b1s{i}", [128, FF], BF16, g67) for i in range(2)]
        self.t_b3s = [sb(f"t_b3s{i}", [128, FF], BF16, g67) for i in range(2)]
        self.t_a2s = sb("t_a2s", [128, FF], BF16, g67)
        self.t_b2s = [sb(f"t_b2s{i}", [128, D], BF16, g67) for i in range(2)]
        self.t_u1 = [sb(f"t_u1{i}", [128, T], BF16, g67) for i in range(2)]
        self.t_u3 = [sb(f"t_u3{i}", [128, T], BF16, g67) for i in range(2)]
        self.t_wfmg = sb("t_wfmg", [NE, 512], BF16, g67)
        g6a = ExitStack()
        self.g6a = g6a
        self.t_wfm = sb("t_wfm", [NE, T], BF16, g6a)
        self.t_a1s = [sb(f"t_a1s{i}", [128, 1024], BF16, g6a) for i in range(2)]
        self.t_a3s = [sb(f"t_a3s{i}", [128, 1024], BF16, g6a) for i in range(2)]


        @block.gpsimd
        def _(gp):
            W(gp, "st", self.st_bc)
            for nm, v in self.spill_p2:
                W(gp, nm, v)
            for src, dst in [(self.a1TA, self.t_a1s[0]), (self.a1TB, self.t_a1s[1]),
                             (self.a3TA, self.t_a3s[0]), (self.a3TB, self.t_a3s[1])]:
                inc("lff", gp.dma_start(
                    out=dst[:].rearrange("p (i c) -> p i c", i=NDC),
                    in_=src.rearrange("(i p) c -> p i c", i=NDC, p=128)), 16)
            for src, dst in [(self.b1PA, self.t_b1s[0]), (self.b1PB, self.t_b1s[1]),
                             (self.b3PA, self.t_b3s[0]), (self.b3PB, self.t_b3s[1])]:
                inc("lff", gp.dma_start(out=dst[:], in_=src[:]), 16)
            inc("lff", gp.dma_start(
                out=self.t_a2s[:].rearrange("p (i c) -> p i c", i=FF // 128),
                in_=self.a2Td.rearrange("(i p) c -> p i c", i=FF // 128, p=128)), 16)
            inc("lff", gp.dma_start(out=self.t_b2s[0][:], in_=self.b2PA[:]), 16)
            inc("lff", gp.dma_start(out=self.t_b2s[1][:], in_=self.b2PB[:]), 16)
        gd_ffnw = V("lff")

        @block.vector
        def _(ve):
            W(ve, "lff", gd_ffnw)
            for ti, t in enumerate([self.t_b1s[0], self.t_b1s[1],
                                    self.t_b3s[0], self.t_b3s[1]]):
                for r in ([0, 32, 64, 96] if ti % 2 == 0 else [0, 32, 64]):
                    inc("ve", ve.tensor_scalar(t[r:r + NR, :], t[r:r + NR, :],
                                               -1.0, None, ALU.mult))
        self.ve_negb = V("ve")

        with ExitStack() as pp:
            p_u = self.ps("p_u", [128, 512], pp)
            first = True
            for tg in range(4):
                @block.sync
                def _(sync, tg=tg, first=first):
                    if first:
                        W(sync, "st", self.st_bc)
                        W(sync, "se", self.se_proj)
                        for nm, v in self.spill_p2:
                            W(sync, nm, v)
                    else:
                        W(sync, "te", V("te"))
                    for dc in range(NDC):
                        inc("ltf", sync.dma_start(
                            out=self.t_tfmg[dc][:],
                            in_=self.sp_tfm[dc][:, tg * 512:(tg + 1) * 512]),
                            16)
                ld_tf = V("ltf")
                for half in range(2):
                    for asrc, udst in [(self.t_a1s, self.t_u1),
                                       (self.t_a3s, self.t_u3)]:
                        @block.tensor
                        def _(te, half=half, tg=tg, asrc=asrc, first=first,
                              ld_tf=ld_tf):
                            W(te, "ltf", ld_tf)
                            if first:
                                W(te, "lff", gd_ffnw)
                                W(te, "se", self.se_proj)
                            else:
                                W(te, "se", V("se"))
                            for dc in range(NDC):
                                i_ = te.matmul(
                                    p_u[:],
                                    asrc[half][:, dc * 128:(dc + 1) * 128],
                                    self.t_tfmg[dc][:],
                                    start=(dc == 0), stop=(dc == NDC - 1))
                            inc("te", i_)
                        te_u = V("te")
                        first = False

                        @block.scalar
                        def _(sc, half=half, tg=tg, udst=udst, te_u=te_u):
                            W(sc, "te", te_u)
                            inc("se", sc.activation(
                                udst[half][:, tg * 512:(tg + 1) * 512], p_u[:],
                                AF.Copy, scale=self.t_twos[:, 0:1]))
            self.se_u = V("se")

        with ExitStack() as pp:
            p_wt = self.ps("p_wt", [NE, 128], pp)
            for tc in range(NTC):
                @block.tensor
                def _(te, tc=tc):
                    W(te, "se", V("se"))
                    i_ = te.transpose(p_wt[:],
                                      self.t_wtm[:, tc * NE:(tc + 1) * NE],
                                      self.t_eyef[:])
                    inc("te", i_)
                te_wt = V("te")

                @block.scalar
                def _(sc, tc=tc, te_wt=te_wt):
                    W(sc, "te", te_wt)
                    inc("se", sc.copy(
                        self.t_wfm[:, tc * 128:(tc + 1) * 128], p_wt[:]))
            self.se_wfm = V("se")

        @block.sync
        def _(sync):
            W(sync, "se", self.se_wfm)
            inc("swf", sync.dma_start(out=self.sp_wfm[:], in_=self.t_wfm[:]),
                16)
        self.st_wfm = V("swf")

        @block.tensor
        def _(te):
            W(te, "swf", self.st_wfm)
        self.g6a.close()

    def p7_ffn_main(self):
        block, inc, V, W = self.block, self.inc, self.V, self.W
        sb, g67 = self.sb, self.g67
        t_w1s = [sb(f"t_w1s{i}", [128, NDC * 128], BF16, g67) for i in range(2)]
        t_w3s = [sb(f"t_w3s{i}", [128, NDC * 128], BF16, g67) for i in range(2)]
        t_w2s = [sb(f"t_w2s{i}", [128, FF // 4], BF16, g67) for i in range(2)]
        t_s1 = [sb(f"t_s1{i}", [128, 512], BF16, g67) for i in range(2)]
        t_t1 = [sb("t_t1", [128, 512], BF16, g67)] * 2
        t_pw = [sb(f"t_pw{i}", [128, 512], BF16, g67) for i in range(2)]
        t_mfm = [sb(f"t_mfm{i}", [128, 512], BF16, g67) for i in range(32)]
        t_wb = [sb(f"t_wb{i}", [128, 512], BF16, g67) for i in range(NE)]
        t_u2w = [sb(f"t_u2w{i}", [128, 512], BF16, g67) for i in range(2)]
        t_g = [sb(f"t_g{i}", [128, 512], BF16, g67) for i in range(2)]

        @block.vector
        def _(ve):
            inc("ve", ve.memset(t_u2w[0][:], 0.0))
            inc("ve", ve.memset(t_u2w[1][:], 0.0))
        ve_u2z = V("ve")

        self.p4_init()
        self.p4_xcr(0)
        self.p4_dt(0)
        self.p4_exps(0)

        gd_w = {}
        te_base1 = {}
        te_base3 = {}
        te_d1 = {}
        te_d3 = {}
        te_u2 = {}
        se_s1 = {}
        ve_pw = {}
        gp_m = {}
        te_w2_last = {}
        ld_tfmg = {}

        for g in range(NTG):
            @block.sync
            def _(sync, g=g):
                if g == 0:
                    W(sync, "te", V("te"))
                    W(sync, "swf", self.st_wfm)
                else:
                    W(sync, "te", te_base3[(g - 1, 31)])
                for dc in range(NDC):
                    inc("ltf", sync.dma_start(
                        out=self.t_tfmg[dc][:],
                        in_=self.sp_tfm[dc][:, g * 512:(g + 1) * 512]), 16)
                inc("ltf", sync.dma_start(
                    out=self.t_wfmg[:],
                    in_=self.sp_wfm[:, g * 512:(g + 1) * 512]), 16)
            ld_tfmg[g] = V("ltf")

            with ExitStack() as ppw:
                p_wb = [self.ps(f"p_wb_{g}_{i}", [128, 512], ppw) for i in range(2)]
                for e in range(NE):
                    @block.tensor
                    def _(te, g=g, e=e):
                        if e == 0:
                            W(te, "se", self.se_wfm)
                            if g == 0:
                                W(te, "se", self.se_u)
                            else:
                                W(te, "se", V("se"))
                                W(te, "ve", V("ve"))
                        if e >= 2:
                            W(te, "se", se_wb[e - 2])
                        if e == 0:
                            W(te, "ltf", ld_tfmg[g])
                        i_ = te.matmul(p_wb[e % 2][:],
                                       self.t_wbsel[:, e * 128:(e + 1) * 128],
                                       self.t_wfmg[:],
                                       start=True, stop=True)
                        inc("te", i_)
                    te_wb = V("te")
                    if e == 0:
                        se_wb = {}

                    @block.scalar
                    def _(sc, g=g, e=e, te_wb=te_wb):
                        W(sc, "te", te_wb)
                        inc("se", sc.copy(t_wb[e][:], p_wb[e % 2][:]))
                    se_wb[e] = V("se")
            se_wb_all = V("se")

            with ExitStack() as ppA:
                p_h1 = [self.ps(f"p_h1_{g}_{i}", [128, 512], ppA)
                        for i in range(2)]
                p_h3 = [self.ps(f"p_h3_{g}_{i}", [128, 512], ppA)
                        for i in range(2)]
                p_U2 = [self.ps(f"p_U2_{g}_{i}", [128, 512], ppA)
                        for i in range(2)]

                def emit_slab(nf, g=g):
                    wb = nf % 2

                    @block.gpsimd
                    def _(gp, g=g, nf=nf, wb=wb):
                        if g == 0 and nf <= 1:
                            W(gp, "swf", self.st_wfm)
                        if nf >= 2:
                            W(gp, "te", te_base3[(g, nf - 2)])
                        elif g >= 1:
                            W(gp, "te", te_base3[(g - 1, 30 + nf)])
                        for wsrc, wdst in [(self.w1T, t_w1s[wb]),
                                           (self.w3T, t_w3s[wb])]:
                            inc(f"gw{wb}", gp.dma_start(
                                out=wdst[:].rearrange("p (i c) -> p i c", i=NDC),
                                in_=wsrc.rearrange("(i p) c -> p i c",
                                                   i=NDC, p=128)[
                                    :, :, nf * 128:(nf + 1) * 128]), 16)
                    gd_w[(g, nf)] = (f"gw{wb}", V(f"gw{wb}"))

                def emit_base(nfsl, br, dlo, dhi, g=g):
                    nwb = nfsl % 2
                    fb = nfsl % 2
                    wsl = t_w1s if br == 0 else t_w3s
                    bsl = self.t_b1s if br == 0 else self.t_b3s
                    usl = self.t_u1 if br == 0 else self.t_u3
                    ph = (p_h1 if br == 0 else p_h3)[fb]

                    @block.tensor
                    def _(te, g=g, nfsl=nfsl, br=br, dlo=dlo, dhi=dhi,
                          nwb=nwb, wsl=wsl, bsl=bsl,
                          usl=usl, ph=ph):
                        if dlo == 0:
                            W(te, *gd_w[(g, nfsl)])
                            if nfsl == 0:
                                W(te, "se", se_wb_all)
                                W(te, "ltf", ld_tfmg[g])
                                if g == 0 and br == 0:
                                    W(te, "ve", self.ve_negb)
                            if nfsl >= 2:
                                if br == 0:
                                    W(te, "se", se_s1[(g, nfsl - 2, NE - 1)])
                                else:
                                    W(te, "ve", ve_pw[(g, nfsl - 2, NE - 1)])
                            elif g >= 1:
                                if br == 0:
                                    W(te, "se",
                                      se_s1[(g - 1, 30 + nfsl, NE - 1)])
                                else:
                                    W(te, "ve",
                                      ve_pw[(g - 1, 30 + nfsl, NE - 1)])
                        for dc in range(dlo, dhi):
                            i_ = te.matmul(
                                ph[:],
                                wsl[nwb][:, dc * 128:(dc + 1) * 128],
                                self.t_tfmg[dc][:],
                                start=(dc == 0), stop=False)
                        if dhi == NDC:
                            i_ = te.matmul(
                                ph[:],
                                bsl[1][96:112, nfsl * 128:(nfsl + 1) * 128],
                                usl[1][96:112, g * 512:(g + 1) * 512],
                                start=False, stop=True,
                                tile_position=(96, 0))
                        inc("te", i_)
                    if dhi == NDC:
                        (te_base1 if br == 0 else te_base3)[(g, nfsl)] = \
                            V("te")

                def emit_d1(fsl, e, g=g):
                    fb = fsl % 2
                    ab = 0 if e <= 4 else 1
                    rb = 32 * ((e - 1) % 4)
                    kwr = dict(tile_position=(96, 0)) if rb == 96 else {}

                    @block.tensor
                    def _(te, g=g, fsl=fsl, e=e, fb=fb, ab=ab, rb=rb, kwr=kwr):
                        W(te, "se", se_s1[(g, fsl, e - 1)])
                        i_ = te.matmul(
                            p_h1[fb][:],
                            self.t_b1s[ab][rb:rb + 32,
                                           fsl * 128:(fsl + 1) * 128],
                            self.t_u1[ab][rb:rb + 32,
                                          g * 512:(g + 1) * 512],
                            start=False, stop=True,
                            skip_group_check=True, **kwr)
                        inc("te", i_)
                    te_d1[(g, fsl, e)] = V("te")

                def emit_d3(fsl, e, g=g):
                    fb = fsl % 2
                    ab = 0 if e <= 4 else 1
                    rb = 32 * ((e - 1) % 4)
                    kwr = dict(tile_position=(96, 0)) if rb == 96 else {}

                    @block.tensor
                    def _(te, g=g, fsl=fsl, e=e, fb=fb, ab=ab, rb=rb, kwr=kwr):
                        W(te, "ve", ve_pw[(g, fsl, e - 1)])
                        i_ = te.matmul(
                            p_h3[fb][:],
                            self.t_b3s[ab][rb:rb + 32,
                                           fsl * 128:(fsl + 1) * 128],
                            self.t_u3[ab][rb:rb + 32,
                                          g * 512:(g + 1) * 512],
                            start=False, stop=True,
                            skip_group_check=True, **kwr)
                        inc("te", i_)
                    te_d3[(g, fsl, e)] = V("te")

                def emit_u2(fsl, e2, g=g):
                    eb2 = e2 % 2
                    half2 = e2 // 4
                    j2 = e2 % 4
                    lo2 = 32 * j2

                    @block.tensor
                    def _(te, g=g, fsl=fsl, e2=e2, eb2=eb2, half2=half2,
                          j2=j2, lo2=lo2):
                        W(te, "ve", ve_pw[(g, fsl, e2)])
                        kwc = (dict(tile_position=(0, 96))
                               if j2 == 3 else {})
                        a2sl = self.t_a2s[:, fsl * 128 + NR * e2:
                                          fsl * 128 + NR * (e2 + 1)]
                        i_ = te.matmul(
                            p_U2[half2][lo2:lo2 + NR, :], a2sl,
                            t_pw[eb2][:],
                            start=(fsl == 0), stop=(fsl == 31),
                            skip_group_check=True, **kwc)
                        inc("te", i_)
                    te_u2[(g, fsl, e2)] = V("te")

                def emit_silu(fsl, e, g=g):
                    fb = fsl % 2
                    eb = e % 2

                    @block.scalar
                    def _(sc, g=g, fsl=fsl, e=e, fb=fb, eb=eb):
                        self.preload_set(sc, [AF.Silu])
                        if e == 0:
                            W(sc, "te", te_base1[(g, fsl)])
                        else:
                            W(sc, "te", te_d1[(g, fsl, e)])
                        key = ((g, fsl, e - 2) if e >= 2 else
                               (g, fsl - 1, e + 6) if fsl >= 1 else
                               (g - 1, 31, e + 6) if g >= 1 else None)
                        if key is not None:
                            W(sc, "ve", ve_pw[key])
                        inc("se", sc.activation(t_s1[eb][:], p_h1[fb][:],
                                                AF.Silu))
                    se_s1[(g, fsl, e)] = V("se")

                def emit_pw(fsl, e, g=g):
                    fb = fsl % 2
                    eb = e % 2

                    @block.vector
                    def _(ve, g=g, fsl=fsl, e=e, fb=fb, eb=eb):
                        W(ve, "se", se_s1[(g, fsl, e)])
                        key = ((g, fsl, e - 2) if e >= 2 else
                               (g, fsl - 1, e + 6) if fsl >= 1 else
                               (g - 1, 31, e + 6) if g >= 1 else None)
                        if key is not None:
                            W(ve, "te", te_u2[key])
                        inc("ve", ve.tensor_mul(t_t1[eb][:], t_s1[eb][:],
                                                t_wb[e][:]))
                        if e == 0:
                            W(ve, "te", te_base3[(g, fsl)])
                        else:
                            W(ve, "te", te_d3[(g, fsl, e)])
                        W(ve, "ve")
                        inc("ve", ve.tensor_mul(t_pw[eb][:], t_t1[eb][:],
                                                p_h3[fb][:]))
                    ve_pw[(g, fsl, e)] = V("ve")

                def emit_macc(fsl, e, g=g):
                    eb = e % 2

                    @block.vector
                    def _(ve, g=g, fsl=fsl, e=e, eb=eb):
                        if fsl == 0 and e == 0 and g >= 1:
                            W(ve, "te", te_w2_last[g - 1])
                        W(ve, "ve")
                        if e == 0:
                            inc("ve", ve.tensor_copy(t_mfm[fsl][:],
                                                     t_pw[eb][:]))
                        else:
                            inc("ve", ve.tensor_add(t_mfm[fsl][:],
                                                    t_mfm[fsl][:],
                                                    t_pw[eb][:]))
                    gp_m[(g, fsl, e)] = V("ve")

                emit_slab(0)
                emit_slab(1)
                emit_base(0, 0, 0, NDC)
                emit_base(0, 1, 0, NDC)
                for fsl in range(32):
                    if fsl + 2 <= 31:
                        emit_slab(fsl + 2)
                    for e in range(NE):
                        if e >= 1:
                            emit_d1(fsl, e)
                            emit_d3(fsl, e)
                            emit_u2(fsl, e - 1)
                        if fsl < 31:
                            if e == 3:
                                emit_base(fsl + 1, 0, 0, 4)
                            elif e == 4:
                                emit_base(fsl + 1, 0, 4, NDC)
                            elif e == 5:
                                emit_base(fsl + 1, 1, 0, 4)
                            elif e == 6:
                                emit_base(fsl + 1, 1, 4, NDC)
                        emit_silu(fsl, e)
                        emit_pw(fsl, e)
                        emit_macc(fsl, e)
                        self.p4_advance(0.75)
                    emit_u2(fsl, NE - 1)

                @block.scalar
                def _(sc, g=g):
                    W(sc, "te", te_u2[(g, 31, NE - 1)])
                    if g == 0:
                        W(sc, "ve", ve_u2z)
                    for half in range(2):
                        for j in range(4):
                            lo = 32 * j
                            inc("se", sc.activation(
                                t_u2w[half][lo:lo + NR, :],
                                p_U2[half][lo:lo + NR, :], AF.Copy,
                                scale=self.t_twos[0:NR, 0:1]))
                se_u2w = V("se")

            with ExitStack() as ppB:
                p_o = [self.ps(f"p_o_{g}_{i}", [128, 512], ppB) for i in range(2)]
                for dcO in range(NDC):
                    ob = dcO % 2
                    if dcO == 0:
                        te_o = {}
                        te_o2 = {}

                    for fh in range(4):
                        @block.gpsimd
                        def _(gp, g=g, dcO=dcO, fh=fh):
                            step2 = dcO * 4 + fh
                            if step2 >= 2:
                                W(gp, "te", te_o2[step2 - 2])
                            elif g >= 1:
                                W(gp, "te", te_o2_prev[4 * NDC - 2 + step2])
                            inc(f"g2w{fh % 2}", gp.dma_start(
                                out=t_w2s[fh % 2][:].rearrange(
                                    "p (i c) -> p i c", i=8),
                                in_=self.w2T.rearrange(
                                    "(i p) c -> p i c", i=32, p=128)[
                                    :, fh * 8:(fh + 1) * 8,
                                    dcO * 128:(dcO + 1) * 128]), 16)
                        gd_w2f = (f"g2w{fh % 2}", V(f"g2w{fh % 2}"))

                        @block.tensor
                        def _(te, g=g, dcO=dcO, ob=ob, fh=fh, gd_w2f=gd_w2f):
                            W(te, *gd_w2f)
                            if dcO == 0 and fh == 0:
                                W(te, "se", se_u2w)
                                W(te, "ve", gp_m[(g, 31, NE - 1)])
                            if fh == 0 and dcO >= 2:
                                W(te, "se", se_g[dcO - 2])
                            for fl in range(8):
                                i_ = te.matmul(
                                    p_o[ob][:],
                                    t_w2s[fh % 2][:, fl * 128:(fl + 1) * 128],
                                    t_mfm[fh * 8 + fl][:],
                                    start=(fh == 0 and fl == 0), stop=False)
                            if fh == 3:
                                te.matmul(p_o[ob][:],
                                          self.t_b2s[0][:,
                                                       dcO * 128:(dcO + 1) * 128],
                                          t_u2w[0][:], start=False, stop=False)
                                i_ = te.matmul(
                                    p_o[ob][:],
                                    self.t_b2s[1][:, dcO * 128:(dcO + 1) * 128],
                                    t_u2w[1][:], start=False, stop=True)
                            inc("te", i_)
                        te_o2[dcO * 4 + fh] = V("te")
                        self.p4_advance(2.0)
                    te_o[dcO] = V("te")
                    if dcO == 0:
                        se_g = {}

                    @block.scalar
                    def _(sc, g=g, dcO=dcO, ob=ob):
                        self.preload_set(sc, [AF.Silu])
                        W(sc, "te", te_o[dcO])
                        if dcO >= 2:
                            W(sc, f"sg{ob}", sg_o[dcO - 2][1])
                        elif g >= 1:
                            W(sc, f"sg{ob}", sg_o_prev[NDC - 2 + dcO][1])
                        inc("se", sc.activation(t_g[ob][:], p_o[ob][:], AF.Silu))
                    se_g[dcO] = V("se")
                    if dcO == 0:
                        sg_o = {}

                    if g == 0 and "gout" in self.taps:
                        o = self.dbg.get("dbg_gout")
                        if o is None:
                            o = self.dout("dbg_gout", (NDC, 128, 512))

                        @block.sync
                        def _(sync, dcO=dcO, ob=ob):
                            W(sync, "se", se_g[dcO])
                            inc("sdbg", sync.dma_start(out=o[dcO],
                                                       in_=t_g[ob][:]), 16)
                        sdbg_g = V("sdbg")

                        @block.vector
                        def _(ve, sdbg_g=sdbg_g):
                            W(ve, "sdbg", sdbg_g)

                    @block.sync
                    def _(sync, g=g, dcO=dcO, ob=ob):
                        W(sync, "se", se_g[dcO])
                        inc(f"sg{ob}", sync.dma_start(
                            out=self.sp_g[g][dcO], in_=t_g[ob][:]), 16)
                    sg_o[dcO] = (f"sg{ob}", V(f"sg{ob}"))
                te_w2_last[g] = te_o[NDC - 1]
                te_o_prev = dict(te_o)
                te_o2_prev = dict(te_o2)
                sg_o_prev = dict(sg_o)
            if g == 0 and "mfm" in self.taps:
                o = self.dout("dbg_mfm", (32, 128, 512), BF16)

                @block.sync
                def _(sync):
                    W(sync, "ve", gp_m[(0, 31, NE - 1)])
                    for i in range(32):
                        inc("sdbg", sync.dma_start(out=o[i], in_=t_mfm[i][:]), 16)
                sdbg_m = V("sdbg")

                @block.tensor
                def _(te):
                    W(te, "sdbg", sdbg_m)
        self.p4_close()

    def p8_resid(self):
        block, inc, V, W = self.block, self.inc, self.V, self.W
        sb = self.sb
        g8 = ExitStack()
        t_gq = [sb(f"t_gq{i}", [128, 512], BF16, g8) for i in range(2)]
        t_sq = [sb(f"t_sq{i}", [128, 512], BF16, g8) for i in range(2)]
        t_xq = [sb(f"t_xq{i}", [128, 512], F32, g8) for i in range(2)]
        t_oq = [sb(f"t_oq{i}", [128, 512], F32, g8) for i in range(2)]

        ve_o = {}
        for step in range(NTG * NDC):
            g, dcO = divmod(step, NDC)
            j = step % 2

            @block.sync
            def _(sync, g=g, dcO=dcO, j=j, step=step):
                if step < 2:
                    W(sync, *self.st_s)
                    W(sync, "sg0", V("sg0"))
                    W(sync, "sg1", V("sg1"))
                    W(sync, "te", V("te"))
                    W(sync, "se", V("se"))
                    W(sync, "ve", V("ve"))
                else:
                    W(sync, "ve", ve_o[step - 2])
                inc(f"lrs{j}", sync.dma_start(out=t_gq[j][:],
                                              in_=self.sp_g[g][dcO]), 16)
                inc(f"lrs{j}", sync.dma_start(
                    out=t_sq[j][:],
                    in_=self.sp_s[dcO][:, g * 512:(g + 1) * 512]), 16)
                inc(f"lrs{j}", sync.dma_start(
                    out=t_xq[j][:],
                    in_=self.x_fm[dcO * 128:(dcO + 1) * 128,
                                  g * 512:(g + 1) * 512]), 16)
            ld_now = (f"lrs{j}", V(f"lrs{j}"))

            @block.vector
            def _(ve, j=j, step=step, ld_now=ld_now):
                W(ve, *ld_now)
                if step >= 2:
                    W(ve, f"out{j}", out_done[step - 2][1])
                inc("ve", ve.tensor_mul(t_oq[j][:], t_sq[j][:], t_gq[j][:]))
                W(ve, "ve")
                inc("ve", ve.tensor_add(t_oq[j][:], t_oq[j][:], t_xq[j][:]))
            ve_o[step] = V("ve")
            if step == 0:
                out_done = {}

            @block.sync
            def _(sync, g=g, dcO=dcO, j=j, step=step):
                W(sync, "ve", ve_o[step])
                inc(f"out{j}", sync.dma_start(
                    out=self.out_fm[dcO * 128:(dcO + 1) * 128,
                                    g * 512:(g + 1) * 512],
                    in_=t_oq[j][:]), 16)
            out_done[step] = (f"out{j}", V(f"out{j}"))

        @block.vector
        def _(ve):
            W(ve, "out0", V("out0"))
            W(ve, "out1", V("out1"))
        g8.close()

    def finish(self):
        block, V, W = self.block, self.V, self.W

        @block.sync
        def _(sync):
            for nm in list(self.sems.keys()):
                if nm[0] in "so" or nm.startswith("out"):
                    W(sync, nm, V(nm))



def _pack_inputs(inp):
    f32 = np.float32
    x = np.asarray(inp["L0_attn_x"])
    ipw = np.asarray(inp["L0_attn_in_proj_w"])
    opw = np.asarray(inp["L0_attn_out_proj_w"])
    cw = np.asarray(inp["L0_attn_conv_w"])
    cb = np.asarray(inp["L0_attn_conv_b"])
    dtw = np.asarray(inp["L0_attn_dt_proj_w"])
    dtb = np.asarray(inp["L0_attn_dt_proj_b"])
    xpw = np.asarray(inp["L0_attn_x_proj_w"])
    alog = np.asarray(inp["L0_attn_A_log"])
    dskip = np.asarray(inp["L0_attn_D_skip"])
    nw = np.asarray(inp["L0_ffn_norm_w"])
    w1 = np.asarray(inp["L0_ffn_w1"])
    w3 = np.asarray(inp["L0_ffn_w3"])
    w2 = np.asarray(inp["L0_ffn_w2"])
    gw = np.asarray(inp["L0_ffn_gate_w"])
    a1 = np.asarray(inp["L0_ffn_a1"])
    b1 = np.asarray(inp["L0_ffn_b1"])
    a3 = np.asarray(inp["L0_ffn_a3"])
    b3 = np.asarray(inp["L0_ffn_b3"])
    a2 = np.asarray(inp["L0_ffn_a2"])
    b2 = np.asarray(inp["L0_ffn_b2"])

    c = np.ascontiguousarray
    sh = {}
    sh["ipT"] = c(ipw.T)
    sh["opT"] = c(opw.T)
    xpT = xpw.T
    xpTp = np.zeros((DI, 112), dtype=xpT.dtype)
    xpTp[:, 0:64] = xpT[:, 0:64]
    xpTp[:, 64:80] = xpT[:, 64:80]
    xpTp[:, 96:112] = xpT[:, 80:96]
    sh["xpTp"] = xpTp
    sh["dtT"] = c(dtw.T)
    sh["A_pack"] = c(alog.reshape(NIC, 128, NST).transpose(1, 0, 2)
                     .reshape(128, NIC * NST))
    sh["dcol"] = c(dskip.reshape(NIC, 128).T.astype(f32))
    sh["cbcol"] = c(cb.reshape(NIC, 128).T.astype(f32))
    sh["dtbcol"] = c(dtb.reshape(NIC, 128).T.astype(f32))
    sh["convw"] = c(cw.reshape(DI, DCONV).reshape(NIC, 128, DCONV)
                    .transpose(1, 0, 2).reshape(128, NIC * DCONV).astype(f32))
    sh["nwp"] = c(nw.reshape(NDC, 128).T.astype(f32))
    sh["gwT"] = c(gw.T.astype(f32))
    sh["w1T"] = c(w1.T.astype(f32))
    sh["w3T"] = c(w3.T.astype(f32))
    sh["w2T"] = c(w2.T.astype(f32))

    def pack_aT(a):
        outA = np.zeros((D, 128), f32)
        outB = np.zeros((D, 128), f32)
        for i, e in enumerate([0, 1, 1, 2, 2, 3, 3, 4]):
            outA[:, 16 * i:16 * i + NR] = a[e].T
        for i, e in enumerate([4, 5, 5, 6, 6, 7]):
            outB[:, 16 * i:16 * i + NR] = a[e].T
        outB[:, 96:112] = a[0].T
        return outA, outB

    sh["a1TA"], sh["a1TB"] = pack_aT(a1.astype(f32))
    sh["a3TA"], sh["a3TB"] = pack_aT(a3.astype(f32))

    def pack_bP(b):
        outA = np.zeros((128, FF), f32)
        outB = np.zeros((128, FF), f32)
        for k in range(4):
            outA[32 * k:32 * k + NR, :] = b[k].T
            outA[32 * k + NR:32 * k + 32, :] = b[k + 1].T
        for k in range(3):
            outB[32 * k:32 * k + NR, :] = b[k + 4].T
            outB[32 * k + NR:32 * k + 32, :] = b[k + 5].T
        outB[96:112, :] = b[0].T
        return outA, outB

    sh["b1PA"], sh["b1PB"] = pack_bP(b1.astype(f32))
    sh["b3PA"], sh["b3PB"] = pack_bP(b3.astype(f32))
    sh["a2T"] = c(a2.reshape(NE * NR, FF).T.astype(f32))

    def pack_b2(bq):
        outA = np.zeros((128, D), f32)
        outB = np.zeros((128, D), f32)
        for j in range(4):
            outA[32 * j:32 * j + NR, :] = bq[j].T
            outB[32 * j:32 * j + NR, :] = bq[j + 4].T
        return outA, outB

    sh["b2PA"], sh["b2PB"] = pack_b2(b2.astype(f32))

    selT = np.zeros((NST, NST * 128), bfdt)
    for n in range(NST):
        selT[n, n * 128:(n + 1) * 128] = 1.0
    sh["selT"] = selT
    wbsel = np.zeros((NE, NE * 128), bfdt)
    for e in range(NE):
        wbsel[e, e * 128:(e + 1) * 128] = 1.0
    sh["wbsel"] = wbsel
    sh["eye_bf"] = np.eye(128, dtype=bfdt)
    sh["eye_f"] = np.eye(128, dtype=f32)
    sh["ones_f"] = np.ones((128, 1), f32)
    sh["ones_b"] = np.ones((128, 1), bfdt)
    sh["onesc_f"] = np.ones((1, 128), f32)
    sh["onesc_b"] = np.ones((1, 128), bfdt)
    sh["epsv"] = np.array([[EPS_LN, EPS_RMS, 0.0, 0.0]], f32)
    sh["twos_f"] = np.full((128, 1), 2.0, f32)

    maps = []
    for b in range(8):
        m = dict(sh)
        m["x_fm"] = c(x[b].T.astype(f32))
        maps.append(m)
    return maps


def kernel(**inputs):
    maps = _pack_inputs(inputs)
    nc, ins, dbg = Build(DEBUG_TAPS).build()
    res = run_bass_kernel_spmd(nc, maps, core_ids=list(range(8)))
    out = np.stack([np.asarray(res.results[b]["out_fm"], np.float32).T
                    for b in range(8)])
    return out
